# version 17
# baseline (speedup 1.0000x reference)
import numpy as np
import ml_dtypes
from contextlib import ExitStack
import concourse.bass as bass
import concourse.mybir as mybir
from concourse.bass_utils import run_bass_kernel_spmd

F32 = mybir.dt.float32
BF16 = mybir.dt.bfloat16
I32 = mybir.dt.int32
I16 = mybir.dt.int16
ALU = mybir.AluOpType
AFT = mybir.ActivationFunctionType
AX = mybir.AxisListType.X

T = 2048
D = 2048
H = 16
HD = 128
MLP = 16384
NCH = 16
TB = 512
NBK = 4
EPS1 = 1e-5
EPS2 = float(np.finfo(np.float32).eps)
ISQ = 0.08838834764831845

DEBUG = 0
DEBUG_STOP = 0


class Sem:
    __slots__ = ("h", "n")
    def __init__(self, h):
        self.h = h
        self.n = 0
    def inc(self, k):
        self.n += k
        return self.n


def build():
    nc = bass.Bass("TRN2", target_bir_lowering=False)

    xT = nc.dram_tensor("xT", [D, T], F32, kind="ExternalInput").ap()
    wq = nc.dram_tensor("wq", [D, D], F32, kind="ExternalInput").ap()
    wkv = nc.dram_tensor("wkv", [D, 2 * D], F32, kind="ExternalInput").ap()
    wo = nc.dram_tensor("wo", [D, D], F32, kind="ExternalInput").ap()
    bq_t = nc.dram_tensor("bq_t", [128, 16], F32, kind="ExternalInput").ap()
    bkvk_t = nc.dram_tensor("bkvk_t", [128, 16], F32, kind="ExternalInput").ap()
    bo_t = nc.dram_tensor("bo_t", [128, 16], F32, kind="ExternalInput").ap()
    bkv_v = nc.dram_tensor("bkv_v", [1, D], F32, kind="ExternalInput").ap()
    gq_t = nc.dram_tensor("gq_t", [128, 16], F32, kind="ExternalInput").ap()
    gkv_t = nc.dram_tensor("gkv_t", [128, 16], F32, kind="ExternalInput").ap()
    ghead = nc.dram_tensor("ghead", [128, 1], F32, kind="ExternalInput").ap()
    pkg = nc.dram_tensor("pkg", [MLP, 1024], I32, kind="ExternalInput").ap()
    scg = nc.dram_tensor("scg", [MLP, 64], F32, kind="ExternalInput").ap()
    pku = nc.dram_tensor("pku", [MLP, 1024], I32, kind="ExternalInput").ap()
    scu = nc.dram_tensor("scu", [MLP, 64], F32, kind="ExternalInput").ap()
    pkd = nc.dram_tensor("pkd", [D, MLP // 2], I32, kind="ExternalInput").ap()
    scd = nc.dram_tensor("scd", [D, MLP // 32], F32, kind="ExternalInput").ap()
    ident = nc.dram_tensor("ident", [128, 128], BF16, kind="ExternalInput").ap()
    ones_c_f = nc.dram_tensor("ones_c_f", [128, 1], F32, kind="ExternalInput").ap()
    ones_r_f = nc.dram_tensor("ones_r_f", [1, 128], F32, kind="ExternalInput").ap()
    ones_c_b = nc.dram_tensor("ones_c_b", [128, 1], BF16, kind="ExternalInput").ap()
    ones_r_b = nc.dram_tensor("ones_r_b", [1, 128], BF16, kind="ExternalInput").ap()
    epsv = nc.dram_tensor("epsv", [1, 2], F32, kind="ExternalInput").ap()

    outT = nc.dram_tensor("outT", [D, T], F32, kind="ExternalOutput").ap()

    wq_d = nc.dram_tensor("wq_d", [D, D], BF16, kind="Internal").ap()
    wo_d = nc.dram_tensor("wo_d", [D, D], BF16, kind="Internal").ap()
    wkv_d = nc.dram_tensor("wkv_d", [D, 2 * D], BF16, kind="Internal").ap()
    wgt_c = nc.dram_tensor("wgt_c", [128, 128, 2048], BF16, kind="Internal").ap()
    wut_c = nc.dram_tensor("wut_c", [128, 128, 2048], BF16, kind="Internal").ap()
    wdt_c = nc.dram_tensor("wdt_c", [128, 128, 2048], BF16, kind="Internal").ap()
    hrms_d = nc.dram_tensor("hrms_d", [D, T], BF16, kind="Internal").ap()
    kT_d = nc.dram_tensor("kT_d", [D, T], BF16, kind="Internal").ap()
    vT_d = nc.dram_tensor("vT_d", [T, D], BF16, kind="Internal").ap()
    x2_d = nc.dram_tensor("x2_d", [D, T], F32, kind="Internal").ap()
    h2_d = nc.dram_tensor("h2_d", [D, T], BF16, kind="Internal").ap()

    dbg = []
    if DEBUG:
        for nm, src, shp, dt in [
                ("dbg_hrms", hrms_d, [D, T], BF16), ("dbg_k", kT_d, [D, T], BF16),
                ("dbg_v", vT_d, [T, D], BF16), ("dbg_x2", x2_d, [D, T], F32),
                ("dbg_h2", h2_d, [D, T], BF16),
                ("dbg_wgt", wgt_c.rearrange("a b c -> (a b) c"), [128 * 128, 2048], BF16),
                ("dbg_wut", wut_c.rearrange("a b c -> (a b) c"), [128 * 128, 2048], BF16),
                ("dbg_wdt", wdt_c.rearrange("a b c -> (a b) c"), [128 * 128, 2048], BF16)]:
            dbg.append((nc.dram_tensor(nm, shp, dt, kind="ExternalOutput").ap(), src))

    es = ExitStack()
    with es:
        e = es.enter_context
        AFa = e(nc.sbuf_tensor("AFa", [128, 12288], F32))
        ABa = e(nc.sbuf_tensor("ABa", [128, 56640], BF16))
        AIa = e(nc.sbuf_tensor("AIa", [128, 16384], I16))

        XLD = [AFa[:, 0:512], AFa[:, 512:1024]]
        SQ = AFa[:, 1024:1536]
        KSQ = AFa[:, 1536:2048]
        MNB = AFa[:, 2048:2560]
        SSB = AFa[:, 2560:3072]
        KRAW = AFa[:, 3072:3584]
        RB = AFa[:, 3584:4096]
        R1 = AFa[0:1, 4096:4608]
        R2 = AFa[0:1, 4608:5120]
        R3 = AFa[0:1, 5120:5632]
        R4 = AFa[0:1, 5632:6144]
        BQ = AFa[:, 6144:6160]
        BKVK = AFa[:, 6160:6176]
        BO = AFa[:, 6176:6192]
        GQ = AFa[:, 6192:6208]
        GKV = AFa[:, 6208:6224]
        GH = AFa[:, 6224:6225]
        SUMS4 = AFa[:, 6228:6232]
        OCF = AFa[:, 6240:6241]
        EPSA = AFa[0:1, 6242:6243]
        EPSB = AFa[0:1, 6243:6244]
        EPSV = AFa[0:1, 6242:6244]
        ORF = AFa[0:1, 6272:6400]
        DSCL = [AFa[:, 9736:9864], AFa[:, 9864:9992]]
        GHR = AFa[0:1, 9992:10120]
        NMX = AFa[:, 6656:6660]
        BIA = AFa[:, 6660:6664]
        BKVV_F = AFa[0:1, 6664:8712]
        GEL = [AFa[:, 8712:9224], AFa[:, 9224:9736]]
        ACC = AFa[:, 0:8192]

        DQSTG = ABa[:, 0:4096]
        DQW = [ABa[:, 4096:8192], ABa[:, 52544:56640]]
        IDN = ABa[:, 8192:8320]
        OCB = ABa[:, 8320:8321]
        ORB = ABa[0:1, 8352:8480]
        BKVV = ABa[0:1, 8480:10528]
        M0 = 10560
        WBUF = [ABa[:, M0 + 0:M0 + 4096], ABa[:, M0 + 4096:M0 + 8192]]
        HB = ABa[:, M0 + 0:M0 + 8192]
        WSL = [ABa[:, M0 + 8192:M0 + 10240], ABa[:, M0 + 10240:M0 + 12288]]
        QH = [ABa[:, M0 + 12288:M0 + 12800], ABa[:, M0 + 12800:M0 + 13312]]
        KH = ABa[:, M0 + 13312:M0 + 15360]
        VH = ABa[:, M0 + 15360:M0 + 17408]
        PSB = ABa[:, M0 + 17408:M0 + 25600]
        PT = ABa[:, M0 + 25600:M0 + 29696]
        WVS = ABa[:, M0 + 29696:M0 + 37888]
        KOUT = ABa[:, M0 + 37888:M0 + 38400]
        VOUT = ABa[:, M0 + 38400:M0 + 38912]
        H2C = [ABa[:, M0 + 38912:M0 + 39424], ABa[:, M0 + 39424:M0 + 39936]]
        VSL = [ABa[:, M0 + 17408:M0 + 25600]]
        KRAWB = ABa[:, M0 + 29696:M0 + 37888]
        SQB2 = [ABa[:, M0 + 25600:M0 + 26624], ABa[:, M0 + 26624:M0 + 27648]]
        KOUT3 = [ABa[:, M0 + 27648:M0 + 28672], ABa[:, M0 + 28672:M0 + 29696]]
        ROWS2 = [AFa[0:1, 4096:5120], AFa[0:1, 5120:6144]]
        DIAG4 = [ABa[:, 8480 + 128 * i:8480 + 128 * (i + 1)] for i in range(4)]
        H2S = ABa[:, M0 + 0:M0 + 8192]
        WG4 = [ABa[:, M0 + 8192:M0 + 10240], ABa[:, M0 + 10240:M0 + 12288]]
        WU4 = [ABa[:, M0 + 12288:M0 + 14336], ABa[:, M0 + 14336:M0 + 16384]]
        WD4 = [ABa[:, M0 + 16384 + 2048 * i:M0 + 16384 + 2048 * (i + 1)]
               for i in range(8)]
        INTER = ABa[:, M0 + 32768:M0 + 36864]

        V16 = [AIa[:, 0:2048], AIa[:, 2048:4096]]
        T1 = AIa[:, 4096:8192]
        T2 = AIa[:, 8192:12288]
        T3 = AIa[:, 12288:16384]

        pbig = e(nc.psum_tensor("pbig", [128, 2048], F32))
        pmid = e(nc.psum_tensor("pmid", [128, 1024], F32))
        ptrq = [e(nc.psum_tensor("ptrE", [128, 1024], BF16)),
                e(nc.psum_tensor("ptrO", [128, 1024], BF16))]
        psml = pmid[:, 0:512]
        ptrqF = [ptrq[0][:, 0:1024].bitcast(F32),
                 ptrq[1][:, 0:1024].bitcast(F32)]

        def mksem(nm):
            return Sem(e(nc.semaphore(nm)))
        sV = mksem("sV")
        sA = mksem("sA")
        sT = mksem("sT")
        sO = mksem("sO")
        sCst = mksem("sCst")
        sWb = [mksem("sWb0"), mksem("sWb1")]
        sPk = [mksem("sPk0"), mksem("sPk1")]
        sXl = [mksem("sXl0"), mksem("sXl1")]
        sHb = mksem("sHb")
        sWs = [mksem("sWs0"), mksem("sWs1")]
        sVsl = mksem("sVsl")
        sKh = mksem("sKh")
        sVh = mksem("sVh")
        sH2s = mksem("sH2s")
        sAcc = mksem("sAcc")
        sWg = [mksem("sWg0"), mksem("sWg1")]
        sWu = [mksem("sWu0"), mksem("sWu1")]
        sWdl = mksem("sWdl")
        sWc = [mksem("sWc0"), mksem("sWc1")]
        sDqw = mksem("sDqw")
        sPl = mksem("sPl")
        sHr = [mksem("sHr0"), mksem("sHr1")]
        sKo = mksem("sKo")
        sVo = mksem("sVo")
        sX2 = mksem("sX2")
        sH2d = [mksem("sH2d0"), mksem("sH2d1")]

        block = e(nc.Block())
        Q = {k: [] for k in ("g", "s", "v", "a", "t")}

        def put(qk, waits, fn, inc=None, inck=1):
            for (s, v) in waits:
                if v and v > s.n:
                    raise AssertionError(
                        f"wait on {s.h.name} for future value {v} > {s.n}")
            val = inc.inc(inck) if inc is not None else None
            wl = tuple((s.h, v) for (s, v) in waits if v and v > 0)
            def run(eng, wl=wl, fn=fn, inc=inc, inck=inck):
                for hh, vv in wl:
                    eng.wait_ge(hh, vv)
                ins = fn(eng)
                if inc is not None:
                    ins.then_inc(inc.h, inck)
            Q[qk].append(run)
            return val

        def dma(qk, out_ap, in_ap, sem, waits=()):
            return put(qk, waits,
                       lambda eng, o=out_ap, i=in_ap: eng.dma_start(out=o, in_=i),
                       inc=sem, inck=16)

        for dst, src in [(BQ, bq_t), (BKVK, bkvk_t), (BO, bo_t), (GQ, gq_t),
                         (GKV, gkv_t), (GH, ghead),
                         (GHR, ghead.rearrange("p o -> o p")), (OCF, ones_c_f),
                         (ORF, ones_r_f), (IDN, ident), (OCB, ones_c_b),
                         (ORB, ones_r_b), (BKVV_F, bkv_v), (EPSV, epsv)]:
            c_consts = dma("s", dst, src, sCst)

        c_bkvv = put("a", [(sCst, c_consts)],
                     lambda a: a.activation(out=BKVV, in_=BKVV_F, func=AFT.Copy),
                     inc=sA)

        wj_done = []
        wj_store = []
        for ji, (kind, c) in enumerate([("q", c) for c in range(NCH)]
                                       + [("kv", c) for c in range(NCH)]
                                       + [("o", c) for c in range(NCH)]):
            bi = ji % 2
            wid = 2048 if kind != "kv" else 4096
            src = {"q": wq, "kv": wkv, "o": wo}[kind]
            dstd = {"q": wq_d, "kv": wkv_d, "o": wo_d}[kind]
            gsb = {"q": GQ, "kv": GKV, "o": None}[kind]
            waits = ([(sV, wj_done[ji - 2]), (sWc[ji % 2], wj_store[ji - 2])]
                     if ji >= 2 else [])
            c_ld = dma("g", WBUF[bi][:, 0:wid], src[c * 128:(c + 1) * 128, :],
                       sWb[bi], waits=waits)
            if gsb is not None:
                c_mul = put("v", [(sWb[bi], c_ld), (sCst, c_consts)],
                            lambda v, b=WBUF[bi], w=wid, g=gsb, c=c:
                            v.tensor_scalar(out=b[:, 0:w], in0=b[:, 0:w],
                                            scalar1=g[:, c:c + 1], scalar2=None,
                                            op0=ALU.mult),
                            inc=sV)
                stw = [(sV, c_mul)]
            else:
                c_mul = sV.n
                stw = [(sWb[bi], c_ld)]
            wj_done.append(c_mul)
            dma("s", dstd[c * 128:(c + 1) * 128, :], WBUF[bi][:, 0:wid],
                sWc[bi], waits=stw)
            wj_store.append(sWc[bi].n)
        c_wcache = [sWc[0].n, sWc[1].n]
        c_wbuf_last_v = wj_done[-1]

        dq_jobs = ([("d", hb, qq) for qq in range(4) for hb in range(16)]
                   + [jj for nb in range(128) for jj in (("g", nb), ("u", nb))])
        dq_v = []
        dq_t = []
        dq_a = []
        dq_pl = []
        dq_store = []
        dq_done = {}
        ptr_hist = []

        def emit_dq_chain(ji):
            job = dq_jobs[ji]
            bi = ji % 2
            if job[0] in ("g", "u"):
                _, nb = job
                npk, ncode, ntile, nsc = 1024, 2048, 16, 64
                pk_ap = (pkg if job[0] == "g" else pku)[nb * 128:(nb + 1) * 128, :]
                sc_ap = (scg if job[0] == "g" else scu)[nb * 128:(nb + 1) * 128, :]
            else:
                _, hb, qq = job
                npk, ncode, ntile, nsc = 2048, 4096, 32, 128
                pk_ap = pkd[hb * 128:(hb + 1) * 128, qq * 2048:(qq + 1) * 2048]
                sc_ap = scd[hb * 128:(hb + 1) * 128, qq * 128:(qq + 1) * 128]

            waits = [(sV, dq_v[ji - 2])] if ji >= 2 else []
            c_l1 = dma("g", V16[bi][:, 0:npk], pk_ap, sPk[bi], waits=waits)
            c_l2 = dma("g", DSCL[bi][:, 0:nsc], sc_ap, sPk[bi])

            waits = [(sPk[bi], c_l2)]
            if ji >= 2:
                waits.append((sT, dq_t[ji - 2]))

            def dq_chain(v, bi=bi, npk=npk, ncode=ncode, nsc=nsc):
                src = V16[bi][:, 0:npk]
                half = ncode // 2
                v.tensor_scalar(out=T1[:, 0:half], in0=src, scalar1=7,
                                scalar2=None, op0=ALU.bitwise_and)
                v.tensor_scalar(out=T1[:, half:ncode], in0=src, scalar1=4,
                                scalar2=7, op0=ALU.logical_shift_right,
                                op1=ALU.bitwise_and)
                v.tensor_scalar(out=T3[:, 0:half], in0=src, scalar1=8,
                                scalar2=12, op0=ALU.bitwise_and,
                                op1=ALU.logical_shift_left)
                v.tensor_scalar(out=T3[:, half:ncode], in0=src, scalar1=128,
                                scalar2=8, op0=ALU.bitwise_and,
                                op1=ALU.logical_shift_left)
                v.tensor_scalar(out=T2[:, 0:ncode], in0=T1[:, 0:ncode],
                                scalar1=252, scalar2=None, op0=ALU.add)
                v.tensor_scalar(out=T1[:, 0:ncode], in0=T1[:, 0:ncode],
                                scalar1=252, scalar2=None, op0=ALU.mult)
                v.tensor_tensor(out=T2[:, 0:ncode], in0=T1[:, 0:ncode],
                                in1=T2[:, 0:ncode], op=ALU.min)
                v.tensor_scalar(out=T2[:, 0:ncode], in0=T2[:, 0:ncode],
                                scalar1=6, scalar2=None,
                                op0=ALU.logical_shift_left)
                v.tensor_tensor(out=DQW[bi].bitcast(I16)[:, 0:ncode],
                                in0=T2[:, 0:ncode],
                                in1=T3[:, 0:ncode], op=ALU.add)
                ins = None
                for hf in range(2):
                    ins = v.tensor_tensor(
                        out=DQW[bi][:, hf * half:(hf + 1) * half].rearrange(
                            "p (b r) -> p b r", r=16),
                        in0=DQW[bi][:, hf * half:(hf + 1) * half].rearrange(
                            "p (b r) -> p b r", r=16),
                        in1=DSCL[bi][:, 0:nsc].unsqueeze(2).broadcast_to(
                            [128, nsc, 16]),
                        op=ALU.mult)
                return ins
            c_dq = put("v", waits, dq_chain, inc=sV)
            dq_v.append(c_dq)
            dq_pl.append(c_dq)

        def emit_dq_pe(ji):
            job = dq_jobs[ji]
            bi = ji % 2
            if job[0] in ("g", "u"):
                ntile = 16
            else:
                ntile = 32
            c_dq = dq_pl[ji]
            ngrp = ntile // 4
            base_a = sA.n
            dq_store_prev = dq_store[ji - 1] if ji >= 1 else 0
            for grp in range(ngrp):
                tw = [(sV, c_dq)] if grp == 0 else []
                if len(ptr_hist) >= 2:
                    tw.append(ptr_hist[-2])

                def tgrp(t, grp=grp, bi=bi):
                    pq = ptrq[grp % 2]
                    ins = None
                    for j in range(4):
                        jj = grp * 4 + j
                        ins = t.transpose(out=pq[:, j * 128:(j + 1) * 128],
                                          in_=DQW[bi][:, jj * 128:(jj + 1) * 128],
                                          identity=IDN)
                    return ins
                c_tr = put("t", tw, tgrp, inc=sT)

                c_te = put("a", [(sT, c_tr), (sDqw, dq_store_prev)],
                           lambda a, grp=grp: a.activation(
                               out=DQSTG[:, grp * 512:(grp + 1) * 512],
                               in_=ptrq[grp % 2][:, 0:512],
                               func=AFT.Copy),
                           inc=sA)
                ptr_hist.append((sA, c_te))
            dq_t.append(sT.n)
            dq_a.append(sA.n)
            c_ev = sA.n

            if job[0] in ("g", "u"):
                cdst = wgt_c if job[0] == "g" else wut_c
                dma("a", cdst[job[1]], DQSTG[:, 0:2048], sDqw, waits=[(sA, c_ev)])
            else:
                _, hb, qq = job
                dst = wdt_c[32 * qq:32 * qq + 32, :, hb * 128:(hb + 1) * 128]
                dst = dst.rearrange("j p h -> p j h")
                dma("a", dst, DQSTG[:, 0:4096].rearrange("p (j h) -> p j h", h=128),
                    sDqw, waits=[(sA, c_ev)])
            dq_store.append(sDqw.n)
            if job[0] == "d" and job[1] == 15:
                dq_done[("dq", job[2])] = sDqw.n
            else:
                dq_done[(job[0], job[1])] = sDqw.n

        dq_ptr = [0]

        def pump_dq(n):
            while n > 0 and dq_ptr[0] < len(dq_jobs) + 1:
                ja = dq_ptr[0]
                if ja < len(dq_jobs):
                    emit_dq_chain(ja)
                if ja >= 1:
                    emit_dq_pe(ja - 1)
                dq_ptr[0] += 1
                n -= 1

        c_bc_prev = 0
        c_st_prev = 0
        c_r_rows_free = 0
        sq_free = 0

        def stats1(v):
            v.tensor_scalar(out=R1, in0=pbig[0:1, 0:512], scalar1=1.0 / D,
                            scalar2=None, op0=ALU.mult)
            v.tensor_scalar(out=R3, in0=pbig[0:1, 512:1024], scalar1=1.0 / D,
                            scalar2=None, op0=ALU.mult)
            v.tensor_tensor(out=R4, in0=R1, in1=R1, op=ALU.mult)
            return v.tensor_tensor(out=R3, in0=R3, in1=R4, op=ALU.subtract)

        def stats2(v):
            v.reciprocal(out=R4, in_=R4)
            v.tensor_tensor(out=R2, in0=R3, in1=R4, op=ALU.mult)
            return v.tensor_tensor(out=R2, in0=R2, in1=R4, op=ALU.mult)

        def stats3(v):
            v.reciprocal(out=R3, in_=R3)
            return v.tensor_tensor(out=R2, in0=R4, in1=R3, op=ALU.mult)

        def bc1(t):
            t.matmul(out=pbig[:, 1024:1536], lhsT=ORF, rhs=R1,
                     start=True, stop=True)
            return t.matmul(out=pbig[:, 1536:2048], lhsT=ORF, rhs=R2,
                            start=True, stop=True)

        def bce(v):
            v.tensor_copy(out=MNB, in_=pbig[:, 1024:1536])
            return v.tensor_copy(out=SSB, in_=pbig[:, 1536:2048])

        def emit_stats_chain(c_stats_mm):
            nonlocal c_bc_prev, c_st_prev, c_r_rows_free
            c_var = put("v", [(sT, c_stats_mm), (sT, c_r_rows_free)], stats1, inc=sV)
            c_st_prev = c_var
            c_sd1 = put("a", [(sV, c_var)],
                        lambda a: a.activation(out=R4, in_=R3, func=AFT.Sqrt,
                                               bias=EPSA), inc=sA)
            c_m2 = put("v", [(sA, c_sd1)], stats2, inc=sV)
            c_sd2 = put("a", [(sV, c_m2)],
                        lambda a: a.activation(out=R3, in_=R2, func=AFT.Sqrt,
                                               bias=EPSB), inc=sA)
            c_s = put("v", [(sA, c_sd2)], stats3, inc=sV)
            c_bc = put("t", [(sV, c_s), (sV, c_bc_prev)], bc1, inc=sT)
            c_r_rows_free = c_bc
            c_bce = put("v", [(sT, c_bc)], bce, inc=sV)
            c_bc_prev = c_bce
            return c_bce

        for tb in range(NBK):
            mm_done = []
            for c in range(NCH):
                bi = c % 2
                waits = [(sT, mm_done[c - 2])] if c >= 2 else []
                c_ld = dma("s", XLD[bi],
                           xT[c * 128:(c + 1) * 128, tb * TB:(tb + 1) * TB],
                           sXl[bi], waits=waits)
                aw = [(sXl[bi], c_ld), (sT, sq_free)]
                c_sq = put("a", aw,
                           lambda a, bi=bi: a.activation(out=SQ, in_=XLD[bi],
                                                         func=AFT.Square),
                           inc=sA)

                def smm(t, bi=bi, c=c):
                    t.matmul(out=pbig[0:1, 0:512], lhsT=OCF, rhs=XLD[bi],
                             start=(c == 0), stop=(c == NCH - 1))
                    return t.matmul(out=pbig[0:1, 512:1024], lhsT=OCF, rhs=SQ,
                                    start=(c == 0), stop=(c == NCH - 1))
                tw = [(sXl[bi], c_ld), (sA, c_sq)]
                if c == 0:
                    tw.append((sV, c_st_prev))
                c_mm = put("t", tw, smm, inc=sT)
                mm_done.append(c_mm)
                sq_free = c_mm
            c_bce_b = emit_stats_chain(mm_done[-1])

            hprev = [0, 0]
            h2w = [0, 0]
            for c in range(NCH):
                bi = c % 2
                c_ld2 = dma("s", XLD[bi],
                            xT[c * 128:(c + 1) * 128, tb * TB:(tb + 1) * TB],
                            sXl[bi], waits=[(sV, c_bce_b), (sV, hprev[bi])])

                def hrm(v, bi=bi):
                    v.tensor_tensor(out=XLD[bi], in0=XLD[bi], in1=MNB,
                                    op=ALU.subtract)
                    return v.tensor_tensor(out=H2C[bi], in0=XLD[bi], in1=SSB,
                                           op=ALU.mult)
                c_h = put("v", [(sXl[bi], c_ld2), (sHr[bi], h2w[bi])], hrm, inc=sV)
                hprev[bi] = c_h
                h2w[bi] = dma("s",
                              hrms_d[c * 128:(c + 1) * 128, tb * TB:(tb + 1) * TB],
                              H2C[bi], sHr[bi], waits=[(sV, c_h)])
                pump_dq(1)
        c_hr_tot = [sHr[0].n, sHr[1].n]

        c_psml_free = 0
        kraw_free = 0
        ksq_free = 0
        k_evac = {}
        v_evac = {}
        kout_w = 0
        vout_w = 0
        hb_users = 0
        wsl_users = [0, 0]
        vsl_users = [0]
        krawb_prev = 0
        kpass2_last = 0
        sqb_free2 = [0, 0]
        rows_sb_free = [0, 0]
        kout_w2 = [0, 0]
        rows_w = [None]
        pr_sq = {}
        pr_rs = {}
        for tb in range(NBK):
            c_hb = dma("s", HB.rearrange("p (c t) -> p c t", t=512),
                       hrms_d[:, tb * TB:(tb + 1) * TB].rearrange(
                           "(c p) t -> p c t", p=128),
                       sHb, waits=[(sHr[0], c_hr_tot[0]), (sHr[1], c_hr_tot[1]),
                                   (sT, hb_users),
                                   (sV, c_wbuf_last_v if tb == 0 else 0)])

            k_mm = {}

            def k_stage1(oc, tb=tb, c_hb=c_hb):
                nonlocal hb_users
                gi = tb * H + oc
                bi = oc % 2
                c_wld = dma("s", WSL[bi].rearrange("p (c w) -> p c w", w=128),
                            wkv_d[:, oc * 128:(oc + 1) * 128].rearrange(
                                "(c p) w -> p c w", p=128),
                            sWs[bi], waits=[(sWc[0], c_wcache[0]),
                                            (sWc[1], c_wcache[1]),
                                            (sT, wsl_users[bi])])
                po = (oc % 4) * 512
                tw = [(sWs[bi], c_wld), (sHb, c_hb)]
                if gi >= 4:
                    tw.append((sA, k_evac[gi - 4]))
                if gi == 0:
                    tw.append((sV, c_bc_prev))
                if oc < 2:
                    tw.append((sV, krawb_prev))

                def kmm(t, bi=bi, po=po):
                    ins = None
                    for c in range(NCH):
                        ins = t.matmul(out=pbig[:, po:po + 512],
                                       lhsT=WSL[bi][:, c * 128:(c + 1) * 128],
                                       rhs=HB[:, c * 512:(c + 1) * 512],
                                       start=(c == 0), stop=(c == NCH - 1))
                    return ins
                c_kmm = put("t", tw, kmm, inc=sT)
                wsl_users[bi] = c_kmm
                hb_users = c_kmm
                k_mm[oc] = c_kmm

            def kev(oc, tb=tb):
                gi = tb * H + oc
                po = (oc % 4) * 512
                c_kev = put("a", [(sT, k_mm[oc]), (sV, krawb_prev)],
                            lambda a, oc=oc, po=po: a.activation(
                                out=KRAWB[:, oc * 512:(oc + 1) * 512],
                                in_=pbig[:, po:po + 512],
                                func=AFT.Identity, bias=BKVK[:, oc:oc + 1]),
                            inc=sA)
                k_evac[gi] = c_kev

            def pair_sq(pp, tb=tb):
                si = pp % 2
                c_sq = put("a", [(sT, sqb_free2[si])],
                           lambda a, pp=pp, si=si: a.activation(
                               out=SQB2[si],
                               in_=KRAWB[:, pp * 1024:(pp + 1) * 1024],
                               func=AFT.Square), inc=sA)
                pr_sq[pp] = c_sq
                pump_dq(2)

            def pair_st(pp, tb=tb):
                si = pp % 2

                def stmm(t, si=si):
                    t.matmul(out=pbig[0:1, 1024:1536], lhsT=OCB,
                             rhs=SQB2[si][:, 0:512], start=True, stop=True)
                    return t.matmul(out=pbig[0:1, 1536:2048], lhsT=OCB,
                                    rhs=SQB2[si][:, 512:1024], start=True,
                                    stop=True)
                c_st = put("t", [(sA, pr_sq[pp]), rows_w[0]], stmm, inc=sT)
                sqb_free2[si] = c_st

                def krs2(a, si=si):
                    a.activation(out=ROWS2[si], in_=pbig[0:1, 1024:2048],
                                 func=AFT.Ln, scale=1.0 / HD, bias=EPSB)
                    return a.activation(out=ROWS2[si], in_=ROWS2[si],
                                        func=AFT.Exp, scale=-0.5)
                c_rs = put("a", [(sT, c_st), (sT, rows_sb_free[si])], krs2,
                           inc=sA)
                rows_w[0] = (sA, c_rs)
                pr_rs[pp] = c_rs

            def pair_bc(pp, tb=tb):
                nonlocal kpass2_last
                si = pp % 2

                def bcst(t, si=si):
                    t.matmul(out=pbig[:, 0:512], lhsT=GHR,
                             rhs=ROWS2[si][0:1, 0:512], start=True, stop=True)
                    return t.matmul(out=pbig[:, 512:1024], lhsT=GHR,
                                    rhs=ROWS2[si][0:1, 512:1024], start=True,
                                    stop=True)
                c_bc2 = put("t", [(sA, pr_rs[pp]), (sV, kpass2_last)], bcst,
                            inc=sT)
                rows_sb_free[si] = c_bc2

                c_ko = put("v", [(sT, c_bc2), (sKo, kout_w2[si])],
                           lambda v, pp=pp, si=si: v.tensor_tensor(
                               out=KOUT3[si],
                               in0=KRAWB[:, pp * 1024:(pp + 1) * 1024],
                               in1=pbig[:, 0:1024], op=ALU.mult), inc=sV)
                kpass2_last = c_ko
                kout_w2[si] = dma(
                    "s",
                    kT_d[2 * pp * 128:(2 * pp + 2) * 128,
                         tb * TB:(tb + 1) * TB].rearrange(
                             "(c p) t -> p c t", p=128),
                    KOUT3[si].rearrange("p (c t) -> p c t", t=512),
                    sKo, waits=[(sV, c_ko)])
                pump_dq(2)

            for oc in range(H):
                k_stage1(oc)
                if oc >= 2:
                    kev(oc - 2)
            kev(H - 2)
            kev(H - 1)

            v_mm = {}
            v_ld = {}

            def v_stage1(idx, tb=tb, c_hb=c_hb):
                nonlocal hb_users
                vc, tch = divmod(idx, 4)
                vi = tb * 16 + idx
                if tch == 0:
                    v_ld[vc] = dma(
                        "s", VSL[0].rearrange("p (c w) -> p c w", w=512),
                        wkv_d[:, D + vc * 512:D + (vc + 1) * 512].rearrange(
                            "(c p) w -> p c w", p=128),
                        sVsl, waits=[(sWc[0], c_wcache[0]),
                                     (sWc[1], c_wcache[1]),
                                     (sT, vsl_users[0])])
                po = 512 if idx % 2 == 0 else 0
                tw = [(sVsl, v_ld[vc]), (sHb, c_hb), (sA, c_bkvv)]
                if vi >= 2:
                    tw.append((sV, v_evac[vi - 2]))

                def vmm(t, tch=tch, vc=vc, po=po):
                    for c in range(NCH):
                        t.matmul(
                            out=pmid[:, po:po + 512],
                            lhsT=HB[:, c * 512 + tch * 128:c * 512 + (tch + 1) * 128],
                            rhs=VSL[0][:, c * 512:(c + 1) * 512],
                            start=(c == 0), stop=False)
                    return t.matmul(out=pmid[:, po:po + 512], lhsT=ORB,
                                    rhs=BKVV[0:1, vc * 512:(vc + 1) * 512],
                                    start=False, stop=True)
                c_vmm = put("t", tw, vmm, inc=sT)
                vsl_users[0] = c_vmm
                hb_users = c_vmm
                v_mm[idx] = c_vmm

            def v_stage2(idx, tb=tb):
                nonlocal vout_w, c_psml_free
                vc, tch = divmod(idx, 4)
                vi = tb * 16 + idx
                po = 512 if idx % 2 == 0 else 0
                c_vev = put("v", [(sT, v_mm[idx]), (sVo, vout_w)],
                            lambda v, po=po: v.tensor_copy(
                                out=VOUT, in_=pmid[:, po:po + 512]),
                            inc=sV)
                v_evac[vi] = c_vev
                if idx % 2 == 1:
                    c_psml_free = c_vev
                tt0 = tb * TB + tch * 128
                vout_w = dma("s", vT_d[tt0:tt0 + 128, vc * 512:(vc + 1) * 512],
                             VOUT, sVo, waits=[(sV, c_vev)])
                if idx % 4 == 3:
                    pump_dq(1)

            if rows_w[0] is None:
                rows_w[0] = (sV, c_bc_prev)
            for step in range(10):
                if step < 8:
                    pair_sq(step)
                if 1 <= step <= 8:
                    pair_st(step - 1)
                if 2 <= step <= 9:
                    pair_bc(step - 2)
                for sub in range(2):
                    idx = 2 * step + sub
                    if idx >= 2 and idx - 2 < 16:
                        v_stage2(idx - 2)
                    if idx < 16:
                        v_stage1(idx)
            krawb_prev = kpass2_last
        c_ko_tot = sKo.n
        c_vo_tot = sVo.n
        v_evac_last = v_evac[NBK * 16 - 1]
        k_evac_last = sA.n
        ph2_v_done = kpass2_last

        c_x2w = 0
        exp_done = {}
        qh_free = [0, 0]
        kh_free = 0
        vh_free = 0
        psb_free = [0] * 4
        ptf = [0, 0]
        wvs_free = 0
        pmidA_free_a = 0
        pmidB_act = 0
        pmidB_free = v_evac_last
        xld_free3 = [0, 0]
        x2c_free = [0, 0]
        mnb_free = 0
        wvs_last_v = 0
        NMX1 = AFa[:, 6236:6237]
        SM1 = AFa[:, 6237:6238]
        RC1 = AFa[:, 6232:6233]
        BIA1 = BIA[:, 0:1]
        prev_sub_key = None
        bia_readers = 0
        sums_readers = 0
        q_chain = {}

        for tb in range(NBK):
            c_hb = dma("s", HB.rearrange("p (c t) -> p c t", t=512),
                       hrms_d[:, tb * TB:(tb + 1) * TB].rearrange(
                           "(c p) t -> p c t", p=128),
                       sHb, waits=[(sHr[0], c_hr_tot[0]), (sHr[1], c_hr_tot[1]),
                                   (sT, hb_users)])
            st = {}

            def load_kv(h, st=st):
                nonlocal kh_free, vh_free
                st[("kh", h)] = dma("s", KH, kT_d[h * 128:(h + 1) * 128, :],
                                    sKh, waits=[(sKo, c_ko_tot), (sT, kh_free)])
                st[("vh", h)] = dma("s", VH.rearrange("p (ck dd) -> p ck dd", dd=128),
                                    vT_d[:, h * 128:(h + 1) * 128].rearrange(
                                        "(ck p) dd -> p ck dd", p=128),
                                    sVh, waits=[(sVo, c_vo_tot), (sT, vh_free)])

            def s1(h, tb=tb, c_hb=c_hb, st=st):
                nonlocal wsl_users, hb_users
                bi = h % 2
                c_wld = dma("s", WSL[bi].rearrange("p (c w) -> p c w", w=128),
                            wq_d[:, h * 128:(h + 1) * 128].rearrange(
                                "(c p) w -> p c w", p=128),
                            sWs[bi], waits=[(sT, wsl_users[bi])])
                tw = [(sWs[bi], c_wld), (sHb, c_hb), (sA, pmidA_free_a),
                      (sV, c_psml_free)]

                def qmm(t, bi=bi):
                    ins = None
                    for c in range(NCH):
                        ins = t.matmul(out=pmid[:, 0:512],
                                       lhsT=WSL[bi][:, c * 128:(c + 1) * 128],
                                       rhs=HB[:, c * 512:(c + 1) * 512],
                                       start=(c == 0), stop=(c == NCH - 1))
                    return ins
                c_qmm = put("t", tw, qmm, inc=sT)
                wsl_users[bi] = c_qmm
                hb_users = c_qmm
                st[("qmm", h)] = c_qmm

            def s2a(h, st=st):
                nonlocal pmidA_free_a, kraw_free

                def qev(a, h=h):
                    a.activation(out=KRAW, in_=pmid[:, 0:512], func=AFT.Identity,
                                 bias=BQ[:, h:h + 1])
                    return a.activation(out=KSQ, in_=KRAW, func=AFT.Square)
                c_qev = put("a", [(sT, st[("qmm", h)]), (sV, kraw_free),
                                  (sT, ksq_free)], qev, inc=sA)
                pmidA_free_a = c_qev
                st[("qev", h)] = c_qev

            def s2b(h, st=st):
                nonlocal ksq_free, c_psml_free, c_r_rows_free, kraw_free
                c_qst = put("t", [(sA, st[("qev", h)]), (sV, c_psml_free)],
                            lambda t: t.matmul(out=psml[0:1, :], lhsT=OCF,
                                               rhs=KSQ, start=True, stop=True),
                            inc=sT)
                ksq_free = c_qst

                def qrs(a):
                    a.activation(out=R1, in_=psml[0:1, :], func=AFT.Ln,
                                 scale=1.0 / HD, bias=EPSB)
                    return a.activation(out=R1, in_=R1, func=AFT.Exp, scale=-0.5)
                c_qrs = put("a", [(sT, c_qst), (sT, c_r_rows_free)], qrs, inc=sA)
                c_qbc = put("t", [(sA, c_qrs)],
                            lambda t: t.matmul(out=psml[:, :], lhsT=ORF, rhs=R1,
                                               start=True, stop=True), inc=sT)
                c_r_rows_free = c_qbc
                c_qh = put("v", [(sT, c_qbc), (sT, qh_free[h % 2])],
                           lambda v, bi=h % 2: v.scalar_tensor_tensor(
                               out=QH[bi], in0=KRAW, scalar=GH[:, 0:1],
                               in1=psml[:, :], op0=ALU.mult, op1=ALU.mult),
                           inc=sV)
                c_psml_free = c_qh
                kraw_free = c_qh
                st[("qh", h)] = c_qh

            def s3a(h, tb=tb, st=st):
                nonlocal qh_free, kh_free, psb_free
                nonlocal bia_readers, sums_readers, prev_sub_key
                bi = h % 2
                gh = tb * H + h
                norm_done = [0, 0, 0, 0]
                for sub in range(4):
                    mx = {}
                    for b in range(4):
                        tw = [(sV, st[("qh", h)]), (sKh, st[("kh", h)])]
                        if prev_sub_key is not None:
                            pk = (prev_sub_key[0], prev_sub_key[1], b)
                            if pk in exp_done:
                                tw.append((sA, exp_done[pk]))
                        else:
                            tw.append((sA, k_evac_last))
                            tw.append((sV, ph2_v_done))
                        if sub == 0 and b == 0:
                            tw.append((sT, psb_free[0]))
                        c_smm = put("t", tw,
                                    lambda t, bi=bi, sub=sub, b=b: t.matmul(
                                        out=pbig[:, b * 512:(b + 1) * 512],
                                        lhsT=QH[bi][:, sub * 128:(sub + 1) * 128],
                                        rhs=KH[:, b * 512:(b + 1) * 512],
                                        start=True, stop=True), inc=sT)
                        mx[b] = put("v", [(sT, c_smm)],
                                    lambda v, b=b: v.reduce_max(
                                        out=NMX[:, b:b + 1],
                                        in_=pbig[:, b * 512:(b + 1) * 512],
                                        axis=AX, negate=True), inc=sV)
                        st[("smm", h, sub, b)] = c_smm
                    qh_free[bi] = st[("smm", h, sub, 3)]
                    kh_free = st[("smm", h, sub, 3)]
                    c_cmb = put("v", [(sV, mx[3]), (sA, bia_readers)],
                                lambda v: v.tensor_reduce(
                                    out=NMX1, in_=NMX[:, 0:4], axis=AX,
                                    op=ALU.min), inc=sV)
                    c_bia = put("v", [(sV, c_cmb)],
                                lambda v: v.tensor_scalar(
                                    out=BIA1, in0=NMX1, scalar1=ISQ,
                                    scalar2=None, op0=ALU.mult), inc=sV)
                    for b in range(4):
                        tw = [(sV, c_bia), (sV, sums_readers),
                              (sT, psb_free[sub])]
                        c_exp = put("a", tw,
                                    lambda a, sub=sub, b=b: a.activation(
                                        out=PSB[:, sub * 2048 + b * 512:
                                                sub * 2048 + (b + 1) * 512],
                                        in_=pbig[:, b * 512:(b + 1) * 512],
                                        func=AFT.Exp, scale=ISQ, bias=BIA1,
                                        accum_out=SUMS4[:, b:b + 1]), inc=sA)
                        exp_done[(gh, sub, b)] = c_exp
                    bia_readers = exp_done[(gh, sub, 3)]
                    c_sum = put("v", [(sA, exp_done[(gh, sub, 3)])],
                                lambda v: v.tensor_reduce(
                                    out=SM1, in_=SUMS4[:, 0:4], axis=AX,
                                    op=ALU.add), inc=sV)
                    sums_readers = c_sum
                    c_rcp = put("v", [(sV, c_sum)],
                                lambda v: v.reciprocal(out=RC1, in_=SM1),
                                inc=sV)
                    c_diag = put("v", [(sV, c_rcp), (sT, psb_free[sub])],
                                 lambda v, sub=sub: v.tensor_scalar(
                                     out=DIAG4[sub], in0=IDN, scalar1=RC1,
                                     scalar2=None, op0=ALU.mult), inc=sV)
                    norm_done[sub] = c_diag
                    prev_sub_key = (gh, sub)
                st[("norm", h)] = norm_done

            def s3b(h, tb=tb, st=st):
                nonlocal vh_free, psb_free
                norm_done = st[("norm", h)]
                te_a = {}
                te_v = {}
                wv_g = {}

                def tr_group(g):
                    nonlocal psb_free
                    for ck in range(4 * g, 4 * g + 4):
                        tw = [(sV, norm_done[0]), (sV, norm_done[1]),
                              (sV, norm_done[2]), (sV, norm_done[3])]
                        if len(ptr_hist) >= 2:
                            tw.append(ptr_hist[-2])

                        def trg(t, ck=ck):
                            pq = ptrqF[ck % 2]
                            ins = None
                            for sub in range(4):
                                ins = t.matmul(
                                    out=pq[:, sub * 128:(sub + 1) * 128],
                                    lhsT=PSB[:, sub * 2048 + ck * 128:
                                             sub * 2048 + (ck + 1) * 128],
                                    rhs=DIAG4[sub], start=True, stop=True)
                            return ins
                        c_tr = put("t", tw, trg, inc=sT)
                        aw = [(sT, c_tr), (sT, ptf[(ck // 4) % 2])]
                        if tb == 0 and h == 0 and ck == 0:
                            aw.append((sKo, c_ko_tot))
                        if ck % 2 == 0:
                            c_te = put("a", aw,
                                       lambda a, ck=ck: a.activation(
                                           out=PT[:, (ck % 8) * 512:
                                                   (ck % 8) * 512 + 512],
                                           in_=ptrqF[ck % 2],
                                           func=AFT.Copy), inc=sA)
                            te_a[ck // 4] = c_te
                            ptr_hist.append((sA, c_te))
                        else:
                            c_te = put("v", aw,
                                       lambda v, ck=ck: v.tensor_copy(
                                           out=PT[:, (ck % 8) * 512:
                                                   (ck % 8) * 512 + 512],
                                           in_=ptrqF[ck % 2]),
                                       inc=sV)
                            te_v[ck // 4] = c_te
                            ptr_hist.append((sV, c_te))
                        if ck == 15:
                            for sub in range(4):
                                psb_free[sub] = c_tr

                def wv_group(g):
                    nonlocal vh_free
                    tw = [(sA, te_a[g]), (sV, te_v[g]), (sVh, st[("vh", h)])]
                    if g == 0:
                        tw.append((sV, pmidB_free))
                        tw.append((sA, pmidB_act))

                    def wvmm(t, g=g):
                        ins = None
                        for ck in range(4 * g, 4 * g + 4):
                            ins = t.matmul(out=pmid[:, 512:1024],
                                           lhsT=VH[:, ck * 128:(ck + 1) * 128],
                                           rhs=PT[:, (ck % 8) * 512:
                                                   (ck % 8) * 512 + 512],
                                           start=(ck == 0), stop=(ck == 15))
                        return ins
                    c = put("t", tw, wvmm, inc=sT)
                    vh_free = c
                    wv_g[g] = c
                    ptf[g % 2] = c

                tr_group(0)
                tr_group(1)
                wv_group(0)
                tr_group(2)
                wv_group(1)
                tr_group(3)
                wv_group(2)
                wv_group(3)
                st[("wv", h)] = wv_g[3]
                if h + 1 < H:
                    load_kv(h + 1)

            def s4(h, st=st):
                nonlocal pmidB_free, wvs_last_v
                c_wvn = put("v", [(sT, st[("wv", h)]), (sT, wvs_free)],
                            lambda v, h=h: v.tensor_copy(
                                out=WVS[:, h * 512:(h + 1) * 512],
                                in_=pmid[:, 512:1024]),
                            inc=sV)
                pmidB_free = c_wvn
                wvs_last_v = c_wvn

            load_kv(0)
            s1(0)
            s2a(0)
            s2b(0)
            for h in range(H):
                s3a(h)
                if h + 1 < H:
                    s1(h + 1)
                    s2a(h + 1)
                    s2b(h + 1)
                s3b(h)
                s4(h)

            op_st = {}

            def op_stage1(oc, tb=tb):
                nonlocal wsl_users
                bi = oc % 2
                c_wldo = dma("s", WSL[bi].rearrange("p (c w) -> p c w", w=128),
                             wo_d[:, oc * 128:(oc + 1) * 128].rearrange(
                                 "(c p) w -> p c w", p=128),
                             sWs[bi], waits=[(sT, wsl_users[bi])])
                op_st[("xld", oc)] = dma(
                    "s", XLD[bi],
                    xT[oc * 128:(oc + 1) * 128, tb * TB:(tb + 1) * TB],
                    sXl[bi], waits=[(sV, xld_free3[bi])])
                po = (oc % 2) * 512
                tw = [(sWs[bi], c_wldo), (sV, wvs_last_v)]
                if oc >= 2:
                    tw.append((sA, op_st[("oev", oc - 2)]))
                else:
                    tw.append((sA, pmidA_free_a))
                    tw.append((sV, pmidB_free))
                if oc == 0:
                    tw.append((sV, c_st_prev))

                def opmm(t, bi=bi, po=po):
                    ins = None
                    for c in range(NCH):
                        ins = t.matmul(out=pmid[:, po:po + 512],
                                       lhsT=WSL[bi][:, c * 128:(c + 1) * 128],
                                       rhs=WVS[:, c * 512:(c + 1) * 512],
                                       start=(c == 0), stop=(c == NCH - 1))
                    return ins
                c_op = put("t", tw, opmm, inc=sT)
                wsl_users[bi] = c_op
                op_st[("mm", oc)] = c_op

            def op_stage2(oc, tb=tb):
                nonlocal kraw_free, c_x2w, mnb_free, x2c_free, xld_free3, wvs_free
                bi = oc % 2
                po = (oc % 2) * 512
                wvs_free = op_st[("mm", oc)]
                c_oev = put("a", [(sT, op_st[("mm", oc)]), (sV, kraw_free)],
                            lambda a, oc=oc, po=po: a.activation(
                                out=KRAW, in_=pmid[:, po:po + 512],
                                func=AFT.Identity, bias=BO[:, oc:oc + 1]),
                            inc=sA)
                op_st[("oev", oc)] = c_oev

                X2C = [SQ, KSQ][bi]
                x2g = [(sA, c_oev), (sXl[bi], op_st[("xld", oc)]),
                       (sT, x2c_free[bi]), (sX2, c_x2w)]
                if oc < 2:
                    x2g.append((sT, sq_free if bi == 0 else ksq_free))
                c_x2 = put("v", x2g,
                           lambda v, bi=bi, X2C=X2C: v.tensor_tensor(
                               out=X2C, in0=XLD[bi], in1=KRAW, op=ALU.add),
                           inc=sV)
                kraw_free = c_x2
                xld_free3[bi] = c_x2
                c_x2w = dma("s",
                            x2_d[oc * 128:(oc + 1) * 128, tb * TB:(tb + 1) * TB],
                            X2C, sX2, waits=[(sV, c_x2)])

                c_sq2 = put("a", [(sV, c_x2), (sT, mnb_free)],
                            lambda a, X2C=X2C: a.activation(out=MNB, in_=X2C,
                                                            func=AFT.Square),
                            inc=sA)

                def smm2(t, oc=oc, X2C=X2C):
                    t.matmul(out=pbig[0:1, 0:512], lhsT=OCF, rhs=X2C,
                             start=(oc == 0), stop=(oc == NCH - 1))
                    return t.matmul(out=pbig[0:1, 512:1024], lhsT=OCF, rhs=MNB,
                                    start=(oc == 0), stop=(oc == NCH - 1))
                tw = [(sA, c_sq2)]
                if oc == 0:
                    tw.append((sA, exp_done[(tb * H + H - 1, 3, 3)]))
                c_mm2 = put("t", tw, smm2, inc=sT)
                mnb_free = c_mm2
                x2c_free[bi] = c_mm2
                if oc % 4 == 3:
                    pump_dq(1)

            for oc in range(NCH + 2):
                if oc >= 2:
                    op_stage2(oc - 2)
                if oc < NCH:
                    op_stage1(oc)
            pmidA_free_a = op_st[("oev", 15)]
            pmidB_act = op_st[("oev", 15)]
            c_stats2 = sT.n

            c_bce_b = emit_stats_chain(c_stats2)

            hprev = [0, 0]
            h2wl = [0, 0]
            for c in range(NCH):
                bi = c % 2
                c_ld2 = dma("s", XLD[bi],
                            x2_d[c * 128:(c + 1) * 128, tb * TB:(tb + 1) * TB],
                            sXl[bi],
                            waits=[(sV, c_bce_b), (sV, hprev[bi]), (sX2, c_x2w)])

                def h2fn(v, bi=bi):
                    v.tensor_tensor(out=XLD[bi], in0=XLD[bi], in1=MNB,
                                    op=ALU.subtract)
                    return v.tensor_tensor(out=H2C[bi], in0=XLD[bi], in1=SSB,
                                           op=ALU.mult)
                c_h = put("v", [(sXl[bi], c_ld2), (sH2d[bi], h2wl[bi])], h2fn, inc=sV)
                hprev[bi] = c_h
                xld_free3[bi] = c_h
                h2wl[bi] = dma("s",
                               h2_d[c * 128:(c + 1) * 128, tb * TB:(tb + 1) * TB],
                               H2C[bi], sH2d[bi], waits=[(sV, c_h)])
                pump_dq(1)
            if DEBUG_STOP:
                break
        c_x2_tot = sX2.n
        c_h2d_tot = [sH2d[0].n, sH2d[1].n]

        endV3 = sV.n
        endA3 = sA.n
        endT3 = sT.n
        pump_dq(10 ** 9)

        acc_add = {}
        gate_mm = {}
        down_mm = {}
        gelu_a = {}
        inter_v = {}
        wg_free = [0, 0]
        wu_free = [0, 0]
        wd_free = 0
        h2s_users = 0
        acc_users = 0
        out_done = 0
        for q4 in range(0 if DEBUG_STOP else 4):
            t0 = q4 * 512
            barrier = ([(sV, endV3), (sA, endA3), (sT, endT3),
                        (sX2, c_x2_tot), (sH2d[0], c_h2d_tot[0]),
                        (sH2d[1], c_h2d_tot[1])] if q4 == 0 else [])
            c_acc = dma("s", ACC.rearrange("p (c t) -> p c t", t=512),
                        x2_d[:, t0:t0 + 512].rearrange("(c p) t -> p c t", p=128),
                        sAcc, waits=barrier + [(sV, acc_users), (sO, out_done)])
            c_h2s = 0
            for kc in range(NCH):
                r0 = 256 * (kc % 8) + (kc // 8)
                src = h2_d[r0:r0 + 255:2, t0:t0 + 512]
                c_h2s = dma("s", H2S[:, kc * 512:(kc + 1) * 512], src,
                            sH2s, waits=barrier + [(sT, h2s_users)])

            for g in range(16):
                for i in range(8):
                    nb = g * 8 + i
                    bi = i % 2
                    c_wg = dma("s", WG4[bi], wgt_c[nb],
                               sWg[bi], waits=[(sDqw, dq_done[("g", nb)]),
                                               (sT, wg_free[bi])])
                    c_wu = dma("s", WU4[bi], wut_c[nb],
                               sWu[bi], waits=[(sDqw, dq_done[("u", nb)]),
                                               (sT, wu_free[bi])])

                    gpo = (i % 2) * 512
                    upo = 1024 + (i % 2) * 512
                    prev = ((q4, g, i - 2) if i >= 2
                            else ((q4, g - 1, i + 6) if g > 0
                                  else ((q4 - 1, 15, i + 6) if q4 > 0 else None)))
                    tw = [(sWg[bi], c_wg), (sH2s, c_h2s)]
                    if prev in gelu_a:
                        tw.append((sA, gelu_a[prev]))

                    def gmm(t, bi=bi, gpo=gpo):
                        ins = None
                        for c in range(NCH):
                            ins = t.matmul(out=pbig[:, gpo:gpo + 512],
                                           lhsT=WG4[bi][:, c * 128:(c + 1) * 128],
                                           rhs=H2S[:, c * 512:(c + 1) * 512],
                                           start=(c == 0), stop=(c == NCH - 1))
                        return ins
                    c_gmm = put("t", tw, gmm, inc=sT)
                    wg_free[bi] = c_gmm
                    gate_mm[(q4, g, i)] = c_gmm

                    tw2 = [(sWu[bi], c_wu), (sH2s, c_h2s)]
                    if prev in inter_v:
                        tw2.append((sV, inter_v[prev]))

                    def umm(t, bi=bi, upo=upo):
                        ins = None
                        for c in range(NCH):
                            ins = t.matmul(out=pbig[:, upo:upo + 512],
                                           lhsT=WU4[bi][:, c * 128:(c + 1) * 128],
                                           rhs=H2S[:, c * 512:(c + 1) * 512],
                                           start=(c == 0), stop=(c == NCH - 1))
                        return ins
                    c_umm = put("t", tw2, umm, inc=sT)
                    wu_free[bi] = c_umm
                    h2s_users = c_umm

                    gel_prev = ((q4, g, i - 2) if i >= 2
                                else ((q4, g - 1, i + 6) if g > 0
                                      else ((q4 - 1, 15, i + 6) if q4 > 0 else None)))
                    tw3 = [(sT, c_gmm)]
                    if gel_prev in inter_v:
                        tw3.append((sV, inter_v[gel_prev]))
                    c_gel = put("a", tw3,
                                lambda a, bi=bi, gpo=gpo: a.activation(
                                    out=GEL[bi], in_=pbig[:, gpo:gpo + 512],
                                    func=AFT.Gelu_apprx_tanh), inc=sA)
                    gelu_a[(q4, g, i)] = c_gel

                    tw4 = [(sA, c_gel), (sT, c_umm)]
                    pg = ((q4, g - 1, 15) if g > 0
                          else ((q4 - 1, 15, 15) if q4 > 0 else None))
                    if pg in down_mm:
                        tw4.append((sT, down_mm[pg]))
                    c_int = put("v", tw4,
                                lambda v, bi=bi, i=i, upo=upo:
                                v.scalar_tensor_tensor(
                                    out=INTER[:, i * 512:(i + 1) * 512],
                                    in0=GEL[bi], scalar=1.0,
                                    in1=pbig[:, upo:upo + 512],
                                    op0=ALU.mult, op1=ALU.mult), inc=sV)
                    inter_v[(q4, g, i)] = c_int

                c_wd = 0
                for i in range(8):
                    nb = g * 8 + i
                    c_wd = dma("s", WD4[i], wdt_c[nb],
                               sWdl, waits=[(sDqw, dq_done[("dq", nb // 32)]),
                                            (sT, wd_free)])

                c_int_all = inter_v[(q4, g, 7)]
                for hc in range(NCH):
                    po = (hc % 2) * 512
                    pa = ((q4, g, hc - 2) if hc >= 2
                          else ((q4, g - 1, 14 + hc) if g > 0
                                else ((q4 - 1, 15, 14 + hc) if q4 > 0 else None)))
                    tw = [(sV, c_int_all), (sWdl, c_wd)]
                    if pa in acc_add:
                        tw.append((sV, acc_add[pa]))

                    def dmm(t, hc=hc, po=po):
                        ins = None
                        for i in range(8):
                            ins = t.matmul(out=pmid[:, po:po + 512],
                                           lhsT=WD4[i][:, hc * 128:(hc + 1) * 128],
                                           rhs=INTER[:, i * 512:(i + 1) * 512],
                                           start=(i == 0), stop=(i == 7))
                        return ins
                    c_dmm = put("t", tw, dmm, inc=sT)
                    down_mm[(q4, g, hc)] = c_dmm
                    wd_free = c_dmm

                    c_add = put("v", [(sT, c_dmm), (sAcc, c_acc)],
                                lambda v, hc=hc, po=po: v.tensor_tensor(
                                    out=ACC[:, hc * 512:(hc + 1) * 512],
                                    in0=ACC[:, hc * 512:(hc + 1) * 512],
                                    in1=pmid[:, po:po + 512], op=ALU.add),
                                inc=sV)
                    acc_add[(q4, g, hc)] = c_add

            for hc in range(NCH):
                out_done = dma("s", outT[hc * 128:(hc + 1) * 128, t0:t0 + 512],
                               ACC[:, hc * 512:(hc + 1) * 512],
                               sO, waits=[(sV, acc_add[(q4, 15, hc)])])
            acc_users = acc_add[(q4, 15, 15)]

        for dap, src in dbg:
            dma("s", dap, src, sO, waits=[(sO, sO.n)])

        put("s", [(sO, sO.n)], lambda s: s.nop(), inc=None)

        @block.gpsimd
        def _(g):
            for f in Q["g"]:
                f(g)

        @block.sync
        def _(s):
            for f in Q["s"]:
                f(s)

        @block.vector
        def _(v):
            for f in Q["v"]:
                f(v)

        @block.scalar
        def _(a):
            for f in Q["a"]:
                f(a)

        @block.tensor
        def _(t):
            for f in Q["t"]:
                f(t)

    return nc


def _sig_n():
    idx = []
    for q in range(4):
        for par in range(2):
            for cblk in range(16):
                base = 4096 * q + 256 * cblk + par
                idx += list(range(base, base + 256, 2))
    return np.array(idx, dtype=np.int64)


def make_inputs(inp, core):
    f32 = np.float32
    x = inp["L0_attn_x"]
    sig = _sig_n()
    bq = np.asarray(inp["L0_attn_bq"], f32)
    bkv = np.asarray(inp["L0_attn_bkv"], f32)
    bo = np.asarray(inp["L0_attn_bo"], f32)
    gq = np.asarray(inp["L0_attn_g_q"], f32)
    gkv = np.asarray(inp["L0_attn_g_kv"], f32)
    return {
        "xT": np.ascontiguousarray(np.asarray(x[core], f32).T),
        "wq": np.ascontiguousarray(np.asarray(inp["L0_attn_Wq"], f32)),
        "wkv": np.ascontiguousarray(np.asarray(inp["L0_attn_Wkv"], f32)),
        "wo": np.ascontiguousarray(np.asarray(inp["L0_attn_Wo"], f32)),
        "bq_t": np.ascontiguousarray(bq.reshape(16, 128).T),
        "bkvk_t": np.ascontiguousarray(bkv[:D].reshape(16, 128).T),
        "bo_t": np.ascontiguousarray(bo.reshape(16, 128).T),
        "bkv_v": np.ascontiguousarray(bkv[D:].reshape(1, D)),
        "gq_t": np.ascontiguousarray(gq.reshape(16, 128).T),
        "gkv_t": np.ascontiguousarray(gkv.reshape(16, 128).T),
        "ghead": np.ascontiguousarray(
            np.asarray(inp["L0_attn_g_head"], f32).reshape(128, 1)),
        "pkg": np.ascontiguousarray(
            np.asarray(inp["L0_ffn_gate_W_packed"], np.int32)[sig]),
        "scg": np.ascontiguousarray(
            np.asarray(inp["L0_ffn_gate_scales"], f32)[sig]),
        "pku": np.ascontiguousarray(
            np.asarray(inp["L0_ffn_up_W_packed"], np.int32)[sig]),
        "scu": np.ascontiguousarray(
            np.asarray(inp["L0_ffn_up_scales"], f32)[sig]),
        "pkd": np.ascontiguousarray(np.asarray(inp["L0_ffn_down_W_packed"], np.int32)),
        "scd": np.ascontiguousarray(np.asarray(inp["L0_ffn_down_scales"], f32)),
        "ident": np.eye(128, dtype=ml_dtypes.bfloat16),
        "ones_c_f": np.ones((128, 1), f32),
        "ones_r_f": np.ones((1, 128), f32),
        "ones_c_b": np.ones((128, 1), ml_dtypes.bfloat16),
        "ones_r_b": np.ones((1, 128), ml_dtypes.bfloat16),
        "epsv": np.array([[EPS1, EPS2]], np.float32),
    }


def kernel(**inputs):
    nc = build()
    in_maps = [make_inputs(inputs, core) for core in range(8)]
    res = run_bass_kernel_spmd(nc, in_maps, core_ids=list(range(8)))
    out = np.stack([np.asarray(r["outT"], dtype=np.float32).T
                    for r in res.results], axis=0)
    return np.ascontiguousarray(out)


# version 18
# speedup vs baseline: 1.1564x; 1.1564x over previous
import numpy as np
import ml_dtypes
from contextlib import ExitStack
import concourse.bass as bass
import concourse.mybir as mybir
from concourse.bass_utils import run_bass_kernel_spmd

F32 = mybir.dt.float32
BF16 = mybir.dt.bfloat16
I32 = mybir.dt.int32
I16 = mybir.dt.int16
ALU = mybir.AluOpType
AFT = mybir.ActivationFunctionType
AX = mybir.AxisListType.X

T = 2048
D = 2048
H = 16
HD = 128
MLP = 16384
NCH = 16
TB = 512
NBK = 4
EPS1 = 1e-5
EPS2 = float(np.finfo(np.float32).eps)
ISQ = 0.08838834764831845

DEBUG = 0
DEBUG_STOP = 0


class Sem:
    __slots__ = ("h", "n")
    def __init__(self, h):
        self.h = h
        self.n = 0
    def inc(self, k):
        self.n += k
        return self.n


def build():
    nc = bass.Bass("TRN2", target_bir_lowering=False)

    xT = nc.dram_tensor("xT", [D, T], F32, kind="ExternalInput").ap()
    wq = nc.dram_tensor("wq", [D, D], F32, kind="ExternalInput").ap()
    wkv = nc.dram_tensor("wkv", [D, 2 * D], F32, kind="ExternalInput").ap()
    wo = nc.dram_tensor("wo", [D, D], F32, kind="ExternalInput").ap()
    bq_t = nc.dram_tensor("bq_t", [128, 16], F32, kind="ExternalInput").ap()
    bkvk_t = nc.dram_tensor("bkvk_t", [128, 16], F32, kind="ExternalInput").ap()
    bo_t = nc.dram_tensor("bo_t", [128, 16], F32, kind="ExternalInput").ap()
    bkv_v = nc.dram_tensor("bkv_v", [1, D], F32, kind="ExternalInput").ap()
    gq_t = nc.dram_tensor("gq_t", [128, 16], F32, kind="ExternalInput").ap()
    gkv_t = nc.dram_tensor("gkv_t", [128, 16], F32, kind="ExternalInput").ap()
    ghead = nc.dram_tensor("ghead", [128, 1], F32, kind="ExternalInput").ap()
    pkg = nc.dram_tensor("pkg", [MLP, 1024], I32, kind="ExternalInput").ap()
    scg = nc.dram_tensor("scg", [MLP, 64], F32, kind="ExternalInput").ap()
    pku = nc.dram_tensor("pku", [MLP, 1024], I32, kind="ExternalInput").ap()
    scu = nc.dram_tensor("scu", [MLP, 64], F32, kind="ExternalInput").ap()
    pkd = nc.dram_tensor("pkd", [D, MLP // 2], I32, kind="ExternalInput").ap()
    scd = nc.dram_tensor("scd", [D, MLP // 32], F32, kind="ExternalInput").ap()
    ident = nc.dram_tensor("ident", [128, 128], BF16, kind="ExternalInput").ap()
    ones_c_f = nc.dram_tensor("ones_c_f", [128, 1], F32, kind="ExternalInput").ap()
    ones_r_f = nc.dram_tensor("ones_r_f", [1, 128], F32, kind="ExternalInput").ap()
    ones_c_b = nc.dram_tensor("ones_c_b", [128, 1], BF16, kind="ExternalInput").ap()
    ones_r_b = nc.dram_tensor("ones_r_b", [1, 128], BF16, kind="ExternalInput").ap()
    epsv = nc.dram_tensor("epsv", [1, 2], F32, kind="ExternalInput").ap()

    outT = nc.dram_tensor("outT", [D, T], F32, kind="ExternalOutput").ap()

    wq_d = nc.dram_tensor("wq_d", [D, D], BF16, kind="Internal").ap()
    wo_d = nc.dram_tensor("wo_d", [D, D], BF16, kind="Internal").ap()
    wkv_d = nc.dram_tensor("wkv_d", [D, 2 * D], BF16, kind="Internal").ap()
    wgt_c = nc.dram_tensor("wgt_c", [128, 128, 2048], BF16, kind="Internal").ap()
    wut_c = nc.dram_tensor("wut_c", [128, 128, 2048], BF16, kind="Internal").ap()
    wdt_c = nc.dram_tensor("wdt_c", [128, 128, 2048], BF16, kind="Internal").ap()
    hrms_d = nc.dram_tensor("hrms_d", [D, T], BF16, kind="Internal").ap()
    kT_d = nc.dram_tensor("kT_d", [D, T], BF16, kind="Internal").ap()
    vT_d = nc.dram_tensor("vT_d", [T, D], BF16, kind="Internal").ap()
    x2_d = nc.dram_tensor("x2_d", [D, T], F32, kind="Internal").ap()
    h2_d = nc.dram_tensor("h2_d", [D, T], BF16, kind="Internal").ap()

    dbg = []
    if DEBUG:
        for nm, src, shp, dt in [
                ("dbg_hrms", hrms_d, [D, T], BF16), ("dbg_k", kT_d, [D, T], BF16),
                ("dbg_v", vT_d, [T, D], BF16), ("dbg_x2", x2_d, [D, T], F32),
                ("dbg_h2", h2_d, [D, T], BF16),
                ("dbg_wgt", wgt_c.rearrange("a b c -> (a b) c"), [128 * 128, 2048], BF16),
                ("dbg_wut", wut_c.rearrange("a b c -> (a b) c"), [128 * 128, 2048], BF16),
                ("dbg_wdt", wdt_c.rearrange("a b c -> (a b) c"), [128 * 128, 2048], BF16)]:
            dbg.append((nc.dram_tensor(nm, shp, dt, kind="ExternalOutput").ap(), src))

    es = ExitStack()
    with es:
        e = es.enter_context
        AFa = e(nc.sbuf_tensor("AFa", [128, 12288], F32))
        ABa = e(nc.sbuf_tensor("ABa", [128, 56640], BF16))
        AIa = e(nc.sbuf_tensor("AIa", [128, 16384], I16))

        XLD = [AFa[:, 0:512], AFa[:, 512:1024]]
        SQ = AFa[:, 1024:1536]
        KSQ = AFa[:, 1536:2048]
        MNB = AFa[:, 2048:2560]
        SSB = AFa[:, 2560:3072]
        KRAW = AFa[:, 3072:3584]
        RB = AFa[:, 3584:4096]
        R1 = AFa[0:1, 4096:4608]
        R2 = AFa[0:1, 4608:5120]
        R3 = AFa[0:1, 5120:5632]
        R4 = AFa[0:1, 5632:6144]
        BQ = AFa[:, 6144:6160]
        BKVK = AFa[:, 6160:6176]
        BO = AFa[:, 6176:6192]
        GQ = AFa[:, 6192:6208]
        GKV = AFa[:, 6208:6224]
        GH = AFa[:, 6224:6225]
        SUMS4 = AFa[:, 6228:6232]
        OCF = AFa[:, 6240:6241]
        EPSA = AFa[0:1, 6242:6243]
        EPSB = AFa[0:1, 6243:6244]
        EPSV = AFa[0:1, 6242:6244]
        ORF = AFa[0:1, 6272:6400]
        DSCL = [AFa[:, 9736:9864], AFa[:, 9864:9992]]
        GHR = AFa[0:1, 9992:10120]
        NMX = AFa[:, 6656:6660]
        BIA = AFa[:, 6660:6664]
        BKVV_F = AFa[0:1, 6664:8712]
        GEL = [AFa[:, 8712:9224], AFa[:, 9224:9736]]
        ACC = AFa[:, 0:8192]

        DQSTG = ABa[:, 0:4096]
        DQW = [ABa[:, 4096:8192], ABa[:, 52544:56640]]
        IDN = ABa[:, 8192:8320]
        OCB = ABa[:, 8320:8321]
        ORB = ABa[0:1, 8352:8480]
        BKVV = ABa[0:1, 8480:10528]
        M0 = 10560
        WBUF = [ABa[:, M0 + 0:M0 + 4096], ABa[:, M0 + 4096:M0 + 8192]]
        HB = ABa[:, M0 + 0:M0 + 8192]
        WSL = [ABa[:, M0 + 8192:M0 + 10240], ABa[:, M0 + 10240:M0 + 12288]]
        QH = [ABa[:, M0 + 12288:M0 + 12800], ABa[:, M0 + 12800:M0 + 13312]]
        KH = ABa[:, M0 + 13312:M0 + 15360]
        VH = ABa[:, M0 + 15360:M0 + 17408]
        PSB = ABa[:, M0 + 17408:M0 + 25600]
        PT = ABa[:, M0 + 25600:M0 + 29696]
        WVS = ABa[:, M0 + 29696:M0 + 37888]
        KOUT = ABa[:, M0 + 37888:M0 + 38400]
        VOUT = ABa[:, M0 + 38400:M0 + 38912]
        H2C = [ABa[:, M0 + 38912:M0 + 39424], ABa[:, M0 + 39424:M0 + 39936]]
        VSL = [ABa[:, M0 + 17408:M0 + 25600]]
        KRAWB = ABa[:, M0 + 29696:M0 + 37888]
        SQB2 = [ABa[:, M0 + 25600:M0 + 26624], ABa[:, M0 + 26624:M0 + 27648]]
        KOUT3 = [ABa[:, M0 + 27648:M0 + 28672], ABa[:, M0 + 28672:M0 + 29696]]
        ROWS2 = [AFa[0:1, 4096:5120], AFa[0:1, 5120:6144]]
        DIAG4 = [ABa[:, 8480 + 128 * i:8480 + 128 * (i + 1)] for i in range(4)]
        H2S = ABa[:, M0 + 0:M0 + 8192]
        WG4 = [ABa[:, M0 + 8192:M0 + 10240], ABa[:, M0 + 10240:M0 + 12288]]
        WU4 = [ABa[:, M0 + 12288:M0 + 14336], ABa[:, M0 + 14336:M0 + 16384]]
        WD4 = [ABa[:, M0 + 16384 + 2048 * i:M0 + 16384 + 2048 * (i + 1)]
               for i in range(8)]
        INTER = ABa[:, M0 + 32768:M0 + 36864]

        V16 = [AIa[:, 0:2048], AIa[:, 2048:4096]]
        T1 = AIa[:, 4096:8192]
        T2 = AIa[:, 8192:12288]
        T3 = AIa[:, 12288:16384]

        pbig = e(nc.psum_tensor("pbig", [128, 2048], F32))
        pmid = e(nc.psum_tensor("pmid", [128, 1024], F32))
        ptrq = [e(nc.psum_tensor("ptrE", [128, 1024], BF16)),
                e(nc.psum_tensor("ptrO", [128, 1024], BF16))]
        psml = pmid[:, 0:512]
        ptrqF = [ptrq[0][:, 0:1024].bitcast(F32),
                 ptrq[1][:, 0:1024].bitcast(F32)]

        def mksem(nm):
            return Sem(e(nc.semaphore(nm)))
        sV = mksem("sV")
        sA = mksem("sA")
        sT = mksem("sT")
        sO = mksem("sO")
        sCst = mksem("sCst")
        sWb = [mksem("sWb0"), mksem("sWb1")]
        sPk = [mksem("sPk0"), mksem("sPk1")]
        sXl = [mksem("sXl0"), mksem("sXl1")]
        sHb = mksem("sHb")
        sWs = [mksem("sWs0"), mksem("sWs1")]
        sVsl = mksem("sVsl")
        sKh = mksem("sKh")
        sVh = mksem("sVh")
        sH2s = mksem("sH2s")
        sAcc = mksem("sAcc")
        sWg = [mksem("sWg0"), mksem("sWg1")]
        sWu = [mksem("sWu0"), mksem("sWu1")]
        sWdl = mksem("sWdl")
        sWc = [mksem("sWc0"), mksem("sWc1")]
        sDqw = mksem("sDqw")
        sPl = mksem("sPl")
        sHr = [mksem("sHr0"), mksem("sHr1")]
        sKo = mksem("sKo")
        sVo = mksem("sVo")
        sX2 = mksem("sX2")
        sH2d = [mksem("sH2d0"), mksem("sH2d1")]

        block = e(nc.Block())
        Q = {k: [] for k in ("g", "s", "v", "a", "t")}

        def put(qk, waits, fn, inc=None, inck=1):
            for (s, v) in waits:
                if v and v > s.n:
                    raise AssertionError(
                        f"wait on {s.h.name} for future value {v} > {s.n}")
            val = inc.inc(inck) if inc is not None else None
            wl = tuple((s.h, v) for (s, v) in waits if v and v > 0)
            def run(eng, wl=wl, fn=fn, inc=inc, inck=inck):
                for hh, vv in wl:
                    eng.wait_ge(hh, vv)
                ins = fn(eng)
                if inc is not None:
                    ins.then_inc(inc.h, inck)
            Q[qk].append(run)
            return val

        def dma(qk, out_ap, in_ap, sem, waits=()):
            return put(qk, waits,
                       lambda eng, o=out_ap, i=in_ap: eng.dma_start(out=o, in_=i),
                       inc=sem, inck=16)

        for dst, src in [(BQ, bq_t), (BKVK, bkvk_t), (BO, bo_t), (GQ, gq_t),
                         (GKV, gkv_t), (GH, ghead),
                         (GHR, ghead.rearrange("p o -> o p")), (OCF, ones_c_f),
                         (ORF, ones_r_f), (IDN, ident), (OCB, ones_c_b),
                         (ORB, ones_r_b), (BKVV_F, bkv_v), (EPSV, epsv)]:
            c_consts = dma("s", dst, src, sCst)

        c_bkvv = put("a", [(sCst, c_consts)],
                     lambda a: a.activation(out=BKVV, in_=BKVV_F, func=AFT.Copy),
                     inc=sA)

        wj_done = []
        wj_store = []
        for ji, (kind, c) in enumerate([("q", c) for c in range(NCH)]
                                       + [("kv", c) for c in range(NCH)]
                                       + [("o", c) for c in range(NCH)]):
            bi = ji % 2
            wid = 2048 if kind != "kv" else 4096
            src = {"q": wq, "kv": wkv, "o": wo}[kind]
            dstd = {"q": wq_d, "kv": wkv_d, "o": wo_d}[kind]
            gsb = {"q": GQ, "kv": GKV, "o": None}[kind]
            waits = ([(sV, wj_done[ji - 2]), (sWc[ji % 2], wj_store[ji - 2])]
                     if ji >= 2 else [])
            c_ld = dma("g", WBUF[bi][:, 0:wid], src[c * 128:(c + 1) * 128, :],
                       sWb[bi], waits=waits)
            if gsb is not None:
                c_mul = put("v", [(sWb[bi], c_ld), (sCst, c_consts)],
                            lambda v, b=WBUF[bi], w=wid, g=gsb, c=c:
                            v.tensor_scalar(out=b[:, 0:w], in0=b[:, 0:w],
                                            scalar1=g[:, c:c + 1], scalar2=None,
                                            op0=ALU.mult),
                            inc=sV)
                stw = [(sV, c_mul)]
            else:
                c_mul = sV.n
                stw = [(sWb[bi], c_ld)]
            wj_done.append(c_mul)
            dma("s", dstd[c * 128:(c + 1) * 128, :], WBUF[bi][:, 0:wid],
                sWc[bi], waits=stw)
            wj_store.append(sWc[bi].n)
        c_wcache = [sWc[0].n, sWc[1].n]
        c_wbuf_last_v = wj_done[-1]

        dq_jobs = ([("d", hb, 0) for hb in range(16)]
                   + [jj for nb in range(128) for jj in (("g", nb), ("u", nb))]
                   + [("d", hb, qq) for qq in (1, 2, 3) for hb in range(16)])
        dq_v = []
        dq_t = []
        dq_a = []
        dq_pl = []
        dq_store = []
        dq_done = {}
        ptr_hist = []

        def emit_dq_chain(ji):
            job = dq_jobs[ji]
            bi = ji % 2
            if job[0] in ("g", "u"):
                _, nb = job
                npk, ncode, ntile, nsc = 1024, 2048, 16, 64
                pk_ap = (pkg if job[0] == "g" else pku)[nb * 128:(nb + 1) * 128, :]
                sc_ap = (scg if job[0] == "g" else scu)[nb * 128:(nb + 1) * 128, :]
            else:
                _, hb, qq = job
                npk, ncode, ntile, nsc = 2048, 4096, 32, 128
                pk_ap = pkd[hb * 128:(hb + 1) * 128, qq * 2048:(qq + 1) * 2048]
                sc_ap = scd[hb * 128:(hb + 1) * 128, qq * 128:(qq + 1) * 128]

            waits = [(sV, dq_v[ji - 2])] if ji >= 2 else []
            c_l1 = dma("g", V16[bi][:, 0:npk], pk_ap, sPk[bi], waits=waits)
            c_l2 = dma("g", DSCL[bi][:, 0:nsc], sc_ap, sPk[bi])

            waits = [(sPk[bi], c_l2)]
            if ji >= 2:
                waits.append((sT, dq_t[ji - 2]))

            def dq_chain(v, bi=bi, npk=npk, ncode=ncode, nsc=nsc):
                src = V16[bi][:, 0:npk]
                half = ncode // 2
                v.tensor_scalar(out=T1[:, 0:half], in0=src, scalar1=7,
                                scalar2=None, op0=ALU.bitwise_and)
                v.tensor_scalar(out=T1[:, half:ncode], in0=src, scalar1=4,
                                scalar2=7, op0=ALU.logical_shift_right,
                                op1=ALU.bitwise_and)
                v.tensor_scalar(out=T3[:, 0:half], in0=src, scalar1=8,
                                scalar2=12, op0=ALU.bitwise_and,
                                op1=ALU.logical_shift_left)
                v.tensor_scalar(out=T3[:, half:ncode], in0=src, scalar1=128,
                                scalar2=8, op0=ALU.bitwise_and,
                                op1=ALU.logical_shift_left)
                v.tensor_scalar(out=T2[:, 0:ncode], in0=T1[:, 0:ncode],
                                scalar1=252, scalar2=None, op0=ALU.add)
                v.tensor_scalar(out=T1[:, 0:ncode], in0=T1[:, 0:ncode],
                                scalar1=252, scalar2=None, op0=ALU.mult)
                v.tensor_tensor(out=T2[:, 0:ncode], in0=T1[:, 0:ncode],
                                in1=T2[:, 0:ncode], op=ALU.min)
                v.tensor_scalar(out=T2[:, 0:ncode], in0=T2[:, 0:ncode],
                                scalar1=6, scalar2=None,
                                op0=ALU.logical_shift_left)
                v.tensor_tensor(out=DQW[bi].bitcast(I16)[:, 0:ncode],
                                in0=T2[:, 0:ncode],
                                in1=T3[:, 0:ncode], op=ALU.add)
                ins = None
                for hf in range(2):
                    ins = v.tensor_tensor(
                        out=DQW[bi][:, hf * half:(hf + 1) * half].rearrange(
                            "p (b r) -> p b r", r=16),
                        in0=DQW[bi][:, hf * half:(hf + 1) * half].rearrange(
                            "p (b r) -> p b r", r=16),
                        in1=DSCL[bi][:, 0:nsc].unsqueeze(2).broadcast_to(
                            [128, nsc, 16]),
                        op=ALU.mult)
                return ins
            c_dq = put("v", waits, dq_chain, inc=sV)
            dq_v.append(c_dq)
            dq_pl.append(c_dq)

        def emit_dq_pe(ji):
            job = dq_jobs[ji]
            bi = ji % 2
            if job[0] in ("g", "u"):
                ntile = 16
            else:
                ntile = 32
            c_dq = dq_pl[ji]
            ngrp = ntile // 4
            base_a = sA.n
            dq_store_prev = dq_store[ji - 1] if ji >= 1 else 0
            for grp in range(ngrp):
                tw = [(sV, c_dq)] if grp == 0 else []
                if len(ptr_hist) >= 2:
                    tw.append(ptr_hist[-2])

                def tgrp(t, grp=grp, bi=bi):
                    pq = ptrq[grp % 2]
                    ins = None
                    for j in range(4):
                        jj = grp * 4 + j
                        ins = t.transpose(out=pq[:, j * 128:(j + 1) * 128],
                                          in_=DQW[bi][:, jj * 128:(jj + 1) * 128],
                                          identity=IDN)
                    return ins
                c_tr = put("t", tw, tgrp, inc=sT)

                c_te = put("a", [(sT, c_tr), (sDqw, dq_store_prev)],
                           lambda a, grp=grp: a.activation(
                               out=DQSTG[:, grp * 512:(grp + 1) * 512],
                               in_=ptrq[grp % 2][:, 0:512],
                               func=AFT.Copy),
                           inc=sA)
                ptr_hist.append((sA, c_te))
            dq_t.append(sT.n)
            dq_a.append(sA.n)
            c_ev = sA.n

            if job[0] in ("g", "u"):
                cdst = wgt_c if job[0] == "g" else wut_c
                dma("a", cdst[job[1]], DQSTG[:, 0:2048], sDqw, waits=[(sA, c_ev)])
            else:
                _, hb, qq = job
                dst = wdt_c[32 * qq:32 * qq + 32, :, hb * 128:(hb + 1) * 128]
                dst = dst.rearrange("j p h -> p j h")
                dma("a", dst, DQSTG[:, 0:4096].rearrange("p (j h) -> p j h", h=128),
                    sDqw, waits=[(sA, c_ev)])
            dq_store.append(sDqw.n)
            if job[0] == "d" and job[1] == 15:
                dq_done[("dq", job[2])] = sDqw.n
            else:
                dq_done[(job[0], job[1])] = sDqw.n

        dq_ptr = [0]

        def pump_dq(n):
            while n > 0 and dq_ptr[0] < len(dq_jobs) + 1:
                ja = dq_ptr[0]
                if ja < len(dq_jobs):
                    emit_dq_chain(ja)
                if ja >= 1:
                    emit_dq_pe(ja - 1)
                dq_ptr[0] += 1
                n -= 1

        c_bc_prev = 0
        c_st_prev = 0
        c_r_rows_free = 0
        sq_free = 0

        def stats1(v):
            v.tensor_scalar(out=R1, in0=pbig[0:1, 0:512], scalar1=1.0 / D,
                            scalar2=None, op0=ALU.mult)
            v.tensor_scalar(out=R3, in0=pbig[0:1, 512:1024], scalar1=1.0 / D,
                            scalar2=None, op0=ALU.mult)
            v.tensor_tensor(out=R4, in0=R1, in1=R1, op=ALU.mult)
            return v.tensor_tensor(out=R3, in0=R3, in1=R4, op=ALU.subtract)

        def stats2(v):
            v.reciprocal(out=R4, in_=R4)
            v.tensor_tensor(out=R2, in0=R3, in1=R4, op=ALU.mult)
            return v.tensor_tensor(out=R2, in0=R2, in1=R4, op=ALU.mult)

        def stats3(v):
            v.reciprocal(out=R3, in_=R3)
            return v.tensor_tensor(out=R2, in0=R4, in1=R3, op=ALU.mult)

        def bc1(t):
            t.matmul(out=pbig[:, 1024:1536], lhsT=ORF, rhs=R1,
                     start=True, stop=True)
            return t.matmul(out=pbig[:, 1536:2048], lhsT=ORF, rhs=R2,
                            start=True, stop=True)

        def bce(v):
            v.tensor_copy(out=MNB, in_=pbig[:, 1024:1536])
            return v.tensor_copy(out=SSB, in_=pbig[:, 1536:2048])

        def emit_stats_chain(c_stats_mm):
            nonlocal c_bc_prev, c_st_prev, c_r_rows_free
            c_var = put("v", [(sT, c_stats_mm), (sT, c_r_rows_free)], stats1, inc=sV)
            c_st_prev = c_var
            c_sd1 = put("a", [(sV, c_var)],
                        lambda a: a.activation(out=R4, in_=R3, func=AFT.Sqrt,
                                               bias=EPSA), inc=sA)
            c_m2 = put("v", [(sA, c_sd1)], stats2, inc=sV)
            c_sd2 = put("a", [(sV, c_m2)],
                        lambda a: a.activation(out=R3, in_=R2, func=AFT.Sqrt,
                                               bias=EPSB), inc=sA)
            c_s = put("v", [(sA, c_sd2)], stats3, inc=sV)
            c_bc = put("t", [(sV, c_s), (sV, c_bc_prev)], bc1, inc=sT)
            c_r_rows_free = c_bc
            c_bce = put("v", [(sT, c_bc)], bce, inc=sV)
            c_bc_prev = c_bce
            return c_bce

        for tb in range(NBK):
            mm_done = []
            for c in range(NCH):
                bi = c % 2
                waits = [(sT, mm_done[c - 2])] if c >= 2 else []
                c_ld = dma("s", XLD[bi],
                           xT[c * 128:(c + 1) * 128, tb * TB:(tb + 1) * TB],
                           sXl[bi], waits=waits)
                aw = [(sXl[bi], c_ld), (sT, sq_free)]
                c_sq = put("a", aw,
                           lambda a, bi=bi: a.activation(out=SQ, in_=XLD[bi],
                                                         func=AFT.Square),
                           inc=sA)

                def smm(t, bi=bi, c=c):
                    t.matmul(out=pbig[0:1, 0:512], lhsT=OCF, rhs=XLD[bi],
                             start=(c == 0), stop=(c == NCH - 1))
                    return t.matmul(out=pbig[0:1, 512:1024], lhsT=OCF, rhs=SQ,
                                    start=(c == 0), stop=(c == NCH - 1))
                tw = [(sXl[bi], c_ld), (sA, c_sq)]
                if c == 0:
                    tw.append((sV, c_st_prev))
                c_mm = put("t", tw, smm, inc=sT)
                mm_done.append(c_mm)
                sq_free = c_mm
            c_bce_b = emit_stats_chain(mm_done[-1])

            hprev = [0, 0]
            h2w = [0, 0]
            for c in range(NCH):
                bi = c % 2
                c_ld2 = dma("s", XLD[bi],
                            xT[c * 128:(c + 1) * 128, tb * TB:(tb + 1) * TB],
                            sXl[bi], waits=[(sV, c_bce_b), (sV, hprev[bi])])

                def hrm(v, bi=bi):
                    v.tensor_tensor(out=XLD[bi], in0=XLD[bi], in1=MNB,
                                    op=ALU.subtract)
                    return v.tensor_tensor(out=H2C[bi], in0=XLD[bi], in1=SSB,
                                           op=ALU.mult)
                c_h = put("v", [(sXl[bi], c_ld2), (sHr[bi], h2w[bi])], hrm, inc=sV)
                hprev[bi] = c_h
                h2w[bi] = dma("s",
                              hrms_d[c * 128:(c + 1) * 128, tb * TB:(tb + 1) * TB],
                              H2C[bi], sHr[bi], waits=[(sV, c_h)])
                if c % 2 == 1:
                    pump_dq(1)
        c_hr_tot = [sHr[0].n, sHr[1].n]

        c_psml_free = 0
        kraw_free = 0
        ksq_free = 0
        k_evac = {}
        v_evac = {}
        kout_w = 0
        vout_w = 0
        hb_users = 0
        wsl_users = [0, 0]
        vsl_users = [0]
        krawb_prev = 0
        kpass2_last = 0
        sqb_free2 = [0, 0]
        rows_sb_free = [0, 0]
        kout_w2 = [0, 0]
        rows_w = [None]
        pr_sq = {}
        pr_rs = {}
        for tb in range(NBK):
            c_hb = dma("s", HB.rearrange("p (c t) -> p c t", t=512),
                       hrms_d[:, tb * TB:(tb + 1) * TB].rearrange(
                           "(c p) t -> p c t", p=128),
                       sHb, waits=[(sHr[0], c_hr_tot[0]), (sHr[1], c_hr_tot[1]),
                                   (sT, hb_users),
                                   (sV, c_wbuf_last_v if tb == 0 else 0)])

            k_mm = {}

            def k_stage1(oc, tb=tb, c_hb=c_hb):
                nonlocal hb_users
                gi = tb * H + oc
                bi = oc % 2
                c_wld = dma("s", WSL[bi].rearrange("p (c w) -> p c w", w=128),
                            wkv_d[:, oc * 128:(oc + 1) * 128].rearrange(
                                "(c p) w -> p c w", p=128),
                            sWs[bi], waits=[(sWc[0], c_wcache[0]),
                                            (sWc[1], c_wcache[1]),
                                            (sT, wsl_users[bi])])
                po = (oc % 4) * 512
                tw = [(sWs[bi], c_wld), (sHb, c_hb)]
                if gi >= 4:
                    tw.append((sA, k_evac[gi - 4]))
                if gi == 0:
                    tw.append((sV, c_bc_prev))
                if oc < 2:
                    tw.append((sV, krawb_prev))

                def kmm(t, bi=bi, po=po):
                    ins = None
                    for c in range(NCH):
                        ins = t.matmul(out=pbig[:, po:po + 512],
                                       lhsT=WSL[bi][:, c * 128:(c + 1) * 128],
                                       rhs=HB[:, c * 512:(c + 1) * 512],
                                       start=(c == 0), stop=(c == NCH - 1))
                    return ins
                c_kmm = put("t", tw, kmm, inc=sT)
                wsl_users[bi] = c_kmm
                hb_users = c_kmm
                k_mm[oc] = c_kmm

            def kev(oc, tb=tb):
                gi = tb * H + oc
                po = (oc % 4) * 512
                c_kev = put("a", [(sT, k_mm[oc]), (sV, krawb_prev)],
                            lambda a, oc=oc, po=po: a.activation(
                                out=KRAWB[:, oc * 512:(oc + 1) * 512],
                                in_=pbig[:, po:po + 512],
                                func=AFT.Identity, bias=BKVK[:, oc:oc + 1]),
                            inc=sA)
                k_evac[gi] = c_kev

            def pair_sq(pp, tb=tb):
                si = pp % 2
                c_sq = put("a", [(sT, sqb_free2[si])],
                           lambda a, pp=pp, si=si: a.activation(
                               out=SQB2[si],
                               in_=KRAWB[:, pp * 1024:(pp + 1) * 1024],
                               func=AFT.Square), inc=sA)
                pr_sq[pp] = c_sq
                pump_dq(1)

            def pair_st(pp, tb=tb):
                si = pp % 2

                def stmm(t, si=si):
                    t.matmul(out=pbig[0:1, 1024:1536], lhsT=OCB,
                             rhs=SQB2[si][:, 0:512], start=True, stop=True)
                    return t.matmul(out=pbig[0:1, 1536:2048], lhsT=OCB,
                                    rhs=SQB2[si][:, 512:1024], start=True,
                                    stop=True)
                c_st = put("t", [(sA, pr_sq[pp]), rows_w[0]], stmm, inc=sT)
                sqb_free2[si] = c_st

                def krs2(a, si=si):
                    a.activation(out=ROWS2[si], in_=pbig[0:1, 1024:2048],
                                 func=AFT.Ln, scale=1.0 / HD, bias=EPSB)
                    return a.activation(out=ROWS2[si], in_=ROWS2[si],
                                        func=AFT.Exp, scale=-0.5)
                c_rs = put("a", [(sT, c_st), (sT, rows_sb_free[si])], krs2,
                           inc=sA)
                rows_w[0] = (sA, c_rs)
                pr_rs[pp] = c_rs

            def pair_bc(pp, tb=tb):
                nonlocal kpass2_last
                si = pp % 2

                def bcst(t, si=si):
                    t.matmul(out=pbig[:, 0:512], lhsT=GHR,
                             rhs=ROWS2[si][0:1, 0:512], start=True, stop=True)
                    return t.matmul(out=pbig[:, 512:1024], lhsT=GHR,
                                    rhs=ROWS2[si][0:1, 512:1024], start=True,
                                    stop=True)
                c_bc2 = put("t", [(sA, pr_rs[pp]), (sV, kpass2_last)], bcst,
                            inc=sT)
                rows_sb_free[si] = c_bc2

                c_ko = put("v", [(sT, c_bc2), (sKo, kout_w2[si])],
                           lambda v, pp=pp, si=si: v.tensor_tensor(
                               out=KOUT3[si],
                               in0=KRAWB[:, pp * 1024:(pp + 1) * 1024],
                               in1=pbig[:, 0:1024], op=ALU.mult), inc=sV)
                kpass2_last = c_ko
                kout_w2[si] = dma(
                    "s",
                    kT_d[2 * pp * 128:(2 * pp + 2) * 128,
                         tb * TB:(tb + 1) * TB].rearrange(
                             "(c p) t -> p c t", p=128),
                    KOUT3[si].rearrange("p (c t) -> p c t", t=512),
                    sKo, waits=[(sV, c_ko)])
                pump_dq(2)

            for oc in range(H):
                k_stage1(oc)
                if oc >= 2:
                    kev(oc - 2)
            kev(H - 2)
            kev(H - 1)

            v_mm = {}
            v_ld = {}

            def v_stage1(idx, tb=tb, c_hb=c_hb):
                nonlocal hb_users
                vc, tch = divmod(idx, 4)
                vi = tb * 16 + idx
                if tch == 0:
                    v_ld[vc] = dma(
                        "s", VSL[0].rearrange("p (c w) -> p c w", w=512),
                        wkv_d[:, D + vc * 512:D + (vc + 1) * 512].rearrange(
                            "(c p) w -> p c w", p=128),
                        sVsl, waits=[(sWc[0], c_wcache[0]),
                                     (sWc[1], c_wcache[1]),
                                     (sT, vsl_users[0])])
                po = 512 if idx % 2 == 0 else 0
                tw = [(sVsl, v_ld[vc]), (sHb, c_hb), (sA, c_bkvv)]
                if vi >= 2:
                    tw.append((sV, v_evac[vi - 2]))

                def vmm(t, tch=tch, vc=vc, po=po):
                    for c in range(NCH):
                        t.matmul(
                            out=pmid[:, po:po + 512],
                            lhsT=HB[:, c * 512 + tch * 128:c * 512 + (tch + 1) * 128],
                            rhs=VSL[0][:, c * 512:(c + 1) * 512],
                            start=(c == 0), stop=False)
                    return t.matmul(out=pmid[:, po:po + 512], lhsT=ORB,
                                    rhs=BKVV[0:1, vc * 512:(vc + 1) * 512],
                                    start=False, stop=True)
                c_vmm = put("t", tw, vmm, inc=sT)
                vsl_users[0] = c_vmm
                hb_users = c_vmm
                v_mm[idx] = c_vmm

            def v_stage2(idx, tb=tb):
                nonlocal vout_w, c_psml_free
                vc, tch = divmod(idx, 4)
                vi = tb * 16 + idx
                po = 512 if idx % 2 == 0 else 0
                c_vev = put("v", [(sT, v_mm[idx]), (sVo, vout_w)],
                            lambda v, po=po: v.tensor_copy(
                                out=VOUT, in_=pmid[:, po:po + 512]),
                            inc=sV)
                v_evac[vi] = c_vev
                if idx % 2 == 1:
                    c_psml_free = c_vev
                tt0 = tb * TB + tch * 128
                vout_w = dma("s", vT_d[tt0:tt0 + 128, vc * 512:(vc + 1) * 512],
                             VOUT, sVo, waits=[(sV, c_vev)])
                if idx % 4 == 3:
                    pump_dq(1)

            if rows_w[0] is None:
                rows_w[0] = (sV, c_bc_prev)
            for step in range(10):
                if step < 8:
                    pair_sq(step)
                if 1 <= step <= 8:
                    pair_st(step - 1)
                if 2 <= step <= 9:
                    pair_bc(step - 2)
                for sub in range(2):
                    idx = 2 * step + sub
                    if idx >= 2 and idx - 2 < 16:
                        v_stage2(idx - 2)
                    if idx < 16:
                        v_stage1(idx)
            krawb_prev = kpass2_last
        c_ko_tot = sKo.n
        c_vo_tot = sVo.n
        v_evac_last = v_evac[NBK * 16 - 1]
        k_evac_last = sA.n
        ph2_v_done = kpass2_last

        c_x2w = 0
        exp_done = {}
        qh_free = [0, 0]
        kh_free = 0
        vh_free = 0
        psb_free = [0] * 4
        ptf = [0, 0]
        wvs_free = 0
        pmidA_free_a = 0
        pmidB_act = 0
        pmidB_free = v_evac_last
        xld_free3 = [0, 0]
        x2c_free = [0, 0]
        mnb_free = 0
        wvs_last_v = 0
        NMX1 = AFa[:, 6236:6237]
        SM1 = AFa[:, 6237:6238]
        RC1 = AFa[:, 6232:6233]
        BIA1 = BIA[:, 0:1]
        prev_sub_key = None
        bia_readers = 0
        sums_readers = 0
        q_chain = {}

        for tb in range(NBK):
            c_hb = dma("s", HB.rearrange("p (c t) -> p c t", t=512),
                       hrms_d[:, tb * TB:(tb + 1) * TB].rearrange(
                           "(c p) t -> p c t", p=128),
                       sHb, waits=[(sHr[0], c_hr_tot[0]), (sHr[1], c_hr_tot[1]),
                                   (sT, hb_users)])
            st = {}

            def load_kv(h, st=st):
                nonlocal kh_free, vh_free
                st[("kh", h)] = dma("s", KH, kT_d[h * 128:(h + 1) * 128, :],
                                    sKh, waits=[(sKo, c_ko_tot), (sT, kh_free)])
                st[("vh", h)] = dma("s", VH.rearrange("p (ck dd) -> p ck dd", dd=128),
                                    vT_d[:, h * 128:(h + 1) * 128].rearrange(
                                        "(ck p) dd -> p ck dd", p=128),
                                    sVh, waits=[(sVo, c_vo_tot), (sT, vh_free)])

            def s1(h, tb=tb, c_hb=c_hb, st=st):
                nonlocal wsl_users, hb_users
                bi = h % 2
                c_wld = dma("s", WSL[bi].rearrange("p (c w) -> p c w", w=128),
                            wq_d[:, h * 128:(h + 1) * 128].rearrange(
                                "(c p) w -> p c w", p=128),
                            sWs[bi], waits=[(sT, wsl_users[bi])])
                tw = [(sWs[bi], c_wld), (sHb, c_hb), (sA, pmidA_free_a),
                      (sV, c_psml_free)]

                def qmm(t, bi=bi):
                    ins = None
                    for c in range(NCH):
                        ins = t.matmul(out=pmid[:, 0:512],
                                       lhsT=WSL[bi][:, c * 128:(c + 1) * 128],
                                       rhs=HB[:, c * 512:(c + 1) * 512],
                                       start=(c == 0), stop=(c == NCH - 1))
                    return ins
                c_qmm = put("t", tw, qmm, inc=sT)
                wsl_users[bi] = c_qmm
                hb_users = c_qmm
                st[("qmm", h)] = c_qmm

            def s2a(h, st=st):
                nonlocal pmidA_free_a, kraw_free

                def qev(a, h=h):
                    a.activation(out=KRAW, in_=pmid[:, 0:512], func=AFT.Identity,
                                 bias=BQ[:, h:h + 1])
                    return a.activation(out=KSQ, in_=KRAW, func=AFT.Square)
                c_qev = put("a", [(sT, st[("qmm", h)]), (sV, kraw_free),
                                  (sT, ksq_free)], qev, inc=sA)
                pmidA_free_a = c_qev
                st[("qev", h)] = c_qev

            def s2b(h, st=st):
                nonlocal ksq_free, c_psml_free, c_r_rows_free, kraw_free
                c_qst = put("t", [(sA, st[("qev", h)]), (sV, c_psml_free)],
                            lambda t: t.matmul(out=psml[0:1, :], lhsT=OCF,
                                               rhs=KSQ, start=True, stop=True),
                            inc=sT)
                ksq_free = c_qst

                def qrs(a):
                    a.activation(out=R1, in_=psml[0:1, :], func=AFT.Ln,
                                 scale=1.0 / HD, bias=EPSB)
                    return a.activation(out=R1, in_=R1, func=AFT.Exp, scale=-0.5)
                c_qrs = put("a", [(sT, c_qst), (sT, c_r_rows_free)], qrs, inc=sA)
                c_qbc = put("t", [(sA, c_qrs)],
                            lambda t: t.matmul(out=psml[:, :], lhsT=ORF, rhs=R1,
                                               start=True, stop=True), inc=sT)
                c_r_rows_free = c_qbc
                c_qh = put("v", [(sT, c_qbc), (sT, qh_free[h % 2])],
                           lambda v, bi=h % 2: v.scalar_tensor_tensor(
                               out=QH[bi], in0=KRAW, scalar=GH[:, 0:1],
                               in1=psml[:, :], op0=ALU.mult, op1=ALU.mult),
                           inc=sV)
                c_psml_free = c_qh
                kraw_free = c_qh
                st[("qh", h)] = c_qh

            def s3a(h, tb=tb, st=st):
                nonlocal qh_free, kh_free, psb_free
                nonlocal bia_readers, sums_readers, prev_sub_key
                bi = h % 2
                gh = tb * H + h
                norm_done = [0, 0, 0, 0]
                for sub in range(4):
                    mx = {}
                    for b in range(4):
                        tw = [(sV, st[("qh", h)]), (sKh, st[("kh", h)])]
                        if prev_sub_key is not None:
                            pk = (prev_sub_key[0], prev_sub_key[1], b)
                            if pk in exp_done:
                                tw.append((sA, exp_done[pk]))
                        else:
                            tw.append((sA, k_evac_last))
                            tw.append((sV, ph2_v_done))
                        if sub == 0 and b == 0:
                            tw.append((sT, psb_free[0]))
                        c_smm = put("t", tw,
                                    lambda t, bi=bi, sub=sub, b=b: t.matmul(
                                        out=pbig[:, b * 512:(b + 1) * 512],
                                        lhsT=QH[bi][:, sub * 128:(sub + 1) * 128],
                                        rhs=KH[:, b * 512:(b + 1) * 512],
                                        start=True, stop=True), inc=sT)
                        mx[b] = put("v", [(sT, c_smm)],
                                    lambda v, b=b: v.reduce_max(
                                        out=NMX[:, b:b + 1],
                                        in_=pbig[:, b * 512:(b + 1) * 512],
                                        axis=AX, negate=True), inc=sV)
                        st[("smm", h, sub, b)] = c_smm
                    qh_free[bi] = st[("smm", h, sub, 3)]
                    kh_free = st[("smm", h, sub, 3)]
                    c_cmb = put("v", [(sV, mx[3]), (sA, bia_readers)],
                                lambda v: v.tensor_reduce(
                                    out=NMX1, in_=NMX[:, 0:4], axis=AX,
                                    op=ALU.min), inc=sV)
                    c_bia = put("v", [(sV, c_cmb)],
                                lambda v: v.tensor_scalar(
                                    out=BIA1, in0=NMX1, scalar1=ISQ,
                                    scalar2=None, op0=ALU.mult), inc=sV)
                    for b in range(4):
                        tw = [(sV, c_bia), (sV, sums_readers),
                              (sT, psb_free[sub])]
                        c_exp = put("a", tw,
                                    lambda a, sub=sub, b=b: a.activation(
                                        out=PSB[:, sub * 2048 + b * 512:
                                                sub * 2048 + (b + 1) * 512],
                                        in_=pbig[:, b * 512:(b + 1) * 512],
                                        func=AFT.Exp, scale=ISQ, bias=BIA1,
                                        accum_out=SUMS4[:, b:b + 1]), inc=sA)
                        exp_done[(gh, sub, b)] = c_exp
                    bia_readers = exp_done[(gh, sub, 3)]
                    c_sum = put("v", [(sA, exp_done[(gh, sub, 3)])],
                                lambda v: v.tensor_reduce(
                                    out=SM1, in_=SUMS4[:, 0:4], axis=AX,
                                    op=ALU.add), inc=sV)
                    sums_readers = c_sum
                    c_rcp = put("v", [(sV, c_sum)],
                                lambda v: v.reciprocal(out=RC1, in_=SM1),
                                inc=sV)
                    c_diag = put("v", [(sV, c_rcp), (sT, psb_free[sub])],
                                 lambda v, sub=sub: v.tensor_scalar(
                                     out=DIAG4[sub], in0=IDN, scalar1=RC1,
                                     scalar2=None, op0=ALU.mult), inc=sV)
                    norm_done[sub] = c_diag
                    prev_sub_key = (gh, sub)
                st[("norm", h)] = norm_done

            def s3b(h, tb=tb, st=st):
                nonlocal vh_free, psb_free
                norm_done = st[("norm", h)]
                te_a = {}
                te_v = {}
                wv_g = {}

                def tr_group(g):
                    nonlocal psb_free
                    for ck in range(4 * g, 4 * g + 4):
                        tw = [(sV, norm_done[0]), (sV, norm_done[1]),
                              (sV, norm_done[2]), (sV, norm_done[3])]
                        if len(ptr_hist) >= 2:
                            tw.append(ptr_hist[-2])

                        def trg(t, ck=ck):
                            pq = ptrqF[ck % 2]
                            ins = None
                            for sub in range(4):
                                ins = t.matmul(
                                    out=pq[:, sub * 128:(sub + 1) * 128],
                                    lhsT=PSB[:, sub * 2048 + ck * 128:
                                             sub * 2048 + (ck + 1) * 128],
                                    rhs=DIAG4[sub], start=True, stop=True)
                            return ins
                        c_tr = put("t", tw, trg, inc=sT)
                        aw = [(sT, c_tr), (sT, ptf[(ck // 4) % 2])]
                        if tb == 0 and h == 0 and ck == 0:
                            aw.append((sKo, c_ko_tot))
                        if ck % 2 == 0:
                            c_te = put("a", aw,
                                       lambda a, ck=ck: a.activation(
                                           out=PT[:, (ck % 8) * 512:
                                                   (ck % 8) * 512 + 512],
                                           in_=ptrqF[ck % 2],
                                           func=AFT.Copy), inc=sA)
                            te_a[ck // 4] = c_te
                            ptr_hist.append((sA, c_te))
                        else:
                            c_te = put("v", aw,
                                       lambda v, ck=ck: v.tensor_copy(
                                           out=PT[:, (ck % 8) * 512:
                                                   (ck % 8) * 512 + 512],
                                           in_=ptrqF[ck % 2]),
                                       inc=sV)
                            te_v[ck // 4] = c_te
                            ptr_hist.append((sV, c_te))
                        if ck == 15:
                            for sub in range(4):
                                psb_free[sub] = c_tr

                def wv_group(g):
                    nonlocal vh_free
                    tw = [(sA, te_a[g]), (sV, te_v[g]), (sVh, st[("vh", h)])]
                    if g == 0:
                        tw.append((sV, pmidB_free))
                        tw.append((sA, pmidB_act))

                    def wvmm(t, g=g):
                        ins = None
                        for ck in range(4 * g, 4 * g + 4):
                            ins = t.matmul(out=pmid[:, 512:1024],
                                           lhsT=VH[:, ck * 128:(ck + 1) * 128],
                                           rhs=PT[:, (ck % 8) * 512:
                                                   (ck % 8) * 512 + 512],
                                           start=(ck == 0), stop=(ck == 15))
                        return ins
                    c = put("t", tw, wvmm, inc=sT)
                    vh_free = c
                    wv_g[g] = c
                    ptf[g % 2] = c

                tr_group(0)
                tr_group(1)
                wv_group(0)
                tr_group(2)
                wv_group(1)
                tr_group(3)
                wv_group(2)
                wv_group(3)
                st[("wv", h)] = wv_g[3]
                if h + 1 < H:
                    load_kv(h + 1)

            def s4(h, st=st):
                nonlocal pmidB_free, wvs_last_v
                c_wvn = put("v", [(sT, st[("wv", h)]), (sT, wvs_free)],
                            lambda v, h=h: v.tensor_copy(
                                out=WVS[:, h * 512:(h + 1) * 512],
                                in_=pmid[:, 512:1024]),
                            inc=sV)
                pmidB_free = c_wvn
                wvs_last_v = c_wvn

            load_kv(0)
            s1(0)
            s2a(0)
            s2b(0)
            for h in range(H):
                s3a(h)
                if h + 1 < H:
                    s1(h + 1)
                    s2a(h + 1)
                    s2b(h + 1)
                s3b(h)
                s4(h)

            op_st = {}

            def op_stage1(oc, tb=tb):
                nonlocal wsl_users
                bi = oc % 2
                c_wldo = dma("s", WSL[bi].rearrange("p (c w) -> p c w", w=128),
                             wo_d[:, oc * 128:(oc + 1) * 128].rearrange(
                                 "(c p) w -> p c w", p=128),
                             sWs[bi], waits=[(sT, wsl_users[bi])])
                op_st[("xld", oc)] = dma(
                    "s", XLD[bi],
                    xT[oc * 128:(oc + 1) * 128, tb * TB:(tb + 1) * TB],
                    sXl[bi], waits=[(sV, xld_free3[bi])])
                po = (oc % 2) * 512
                tw = [(sWs[bi], c_wldo), (sV, wvs_last_v)]
                if oc >= 2:
                    tw.append((sA, op_st[("oev", oc - 2)]))
                else:
                    tw.append((sA, pmidA_free_a))
                    tw.append((sV, pmidB_free))
                if oc == 0:
                    tw.append((sV, c_st_prev))

                def opmm(t, bi=bi, po=po):
                    ins = None
                    for c in range(NCH):
                        ins = t.matmul(out=pmid[:, po:po + 512],
                                       lhsT=WSL[bi][:, c * 128:(c + 1) * 128],
                                       rhs=WVS[:, c * 512:(c + 1) * 512],
                                       start=(c == 0), stop=(c == NCH - 1))
                    return ins
                c_op = put("t", tw, opmm, inc=sT)
                wsl_users[bi] = c_op
                op_st[("mm", oc)] = c_op

            def op_stage2(oc, tb=tb):
                nonlocal kraw_free, c_x2w, mnb_free, x2c_free, xld_free3, wvs_free
                bi = oc % 2
                po = (oc % 2) * 512
                wvs_free = op_st[("mm", oc)]
                c_oev = put("a", [(sT, op_st[("mm", oc)]), (sV, kraw_free)],
                            lambda a, oc=oc, po=po: a.activation(
                                out=KRAW, in_=pmid[:, po:po + 512],
                                func=AFT.Identity, bias=BO[:, oc:oc + 1]),
                            inc=sA)
                op_st[("oev", oc)] = c_oev

                X2C = [SQ, KSQ][bi]
                x2g = [(sA, c_oev), (sXl[bi], op_st[("xld", oc)]),
                       (sT, x2c_free[bi]), (sX2, c_x2w)]
                if oc < 2:
                    x2g.append((sT, sq_free if bi == 0 else ksq_free))
                c_x2 = put("v", x2g,
                           lambda v, bi=bi, X2C=X2C: v.tensor_tensor(
                               out=X2C, in0=XLD[bi], in1=KRAW, op=ALU.add),
                           inc=sV)
                kraw_free = c_x2
                xld_free3[bi] = c_x2
                c_x2w = dma("s",
                            x2_d[oc * 128:(oc + 1) * 128, tb * TB:(tb + 1) * TB],
                            X2C, sX2, waits=[(sV, c_x2)])

                c_sq2 = put("a", [(sV, c_x2), (sT, mnb_free)],
                            lambda a, X2C=X2C: a.activation(out=MNB, in_=X2C,
                                                            func=AFT.Square),
                            inc=sA)

                def smm2(t, oc=oc, X2C=X2C):
                    t.matmul(out=pbig[0:1, 0:512], lhsT=OCF, rhs=X2C,
                             start=(oc == 0), stop=(oc == NCH - 1))
                    return t.matmul(out=pbig[0:1, 512:1024], lhsT=OCF, rhs=MNB,
                                    start=(oc == 0), stop=(oc == NCH - 1))
                tw = [(sA, c_sq2)]
                if oc == 0:
                    tw.append((sA, exp_done[(tb * H + H - 1, 3, 3)]))
                c_mm2 = put("t", tw, smm2, inc=sT)
                mnb_free = c_mm2
                x2c_free[bi] = c_mm2
                pump_dq(1)

            for oc in range(NCH + 2):
                if oc >= 2:
                    op_stage2(oc - 2)
                if oc < NCH:
                    op_stage1(oc)
            pmidA_free_a = op_st[("oev", 15)]
            pmidB_act = op_st[("oev", 15)]
            c_stats2 = sT.n

            c_bce_b = emit_stats_chain(c_stats2)

            hprev = [0, 0]
            h2wl = [0, 0]
            for c in range(NCH):
                bi = c % 2
                c_ld2 = dma("s", XLD[bi],
                            x2_d[c * 128:(c + 1) * 128, tb * TB:(tb + 1) * TB],
                            sXl[bi],
                            waits=[(sV, c_bce_b), (sV, hprev[bi]), (sX2, c_x2w)])

                def h2fn(v, bi=bi):
                    v.tensor_tensor(out=XLD[bi], in0=XLD[bi], in1=MNB,
                                    op=ALU.subtract)
                    return v.tensor_tensor(out=H2C[bi], in0=XLD[bi], in1=SSB,
                                           op=ALU.mult)
                c_h = put("v", [(sXl[bi], c_ld2), (sH2d[bi], h2wl[bi])], h2fn, inc=sV)
                hprev[bi] = c_h
                xld_free3[bi] = c_h
                h2wl[bi] = dma("s",
                               h2_d[c * 128:(c + 1) * 128, tb * TB:(tb + 1) * TB],
                               H2C[bi], sH2d[bi], waits=[(sV, c_h)])
                pump_dq(1)
            if DEBUG_STOP:
                break
        c_x2_tot = sX2.n
        c_h2d_tot = [sH2d[0].n, sH2d[1].n]

        endV3 = sV.n
        endA3 = sA.n
        endT3 = sT.n
        pump_dq(max(0, 273 - dq_ptr[0]))

        acc_add = {}
        gate_mm = {}
        down_mm = {}
        gelu_a = {}
        inter_v = {}
        wg_free = [0, 0]
        wu_free = [0, 0]
        wd_free = 0
        h2s_users = 0
        acc_users = 0
        out_done = 0
        for q4 in range(0 if DEBUG_STOP else 4):
            t0 = q4 * 512
            barrier = ([(sV, endV3), (sA, endA3), (sT, endT3),
                        (sX2, c_x2_tot), (sH2d[0], c_h2d_tot[0]),
                        (sH2d[1], c_h2d_tot[1])] if q4 == 0 else [])
            c_acc = dma("s", ACC.rearrange("p (c t) -> p c t", t=512),
                        x2_d[:, t0:t0 + 512].rearrange("(c p) t -> p c t", p=128),
                        sAcc, waits=barrier + [(sV, acc_users), (sO, out_done)])
            c_h2s = 0
            for kc in range(NCH):
                r0 = 256 * (kc % 8) + (kc // 8)
                src = h2_d[r0:r0 + 255:2, t0:t0 + 512]
                c_h2s = dma("s", H2S[:, kc * 512:(kc + 1) * 512], src,
                            sH2s, waits=barrier + [(sT, h2s_users)])

            for g in range(16):
                if q4 == 0 and g < 12:
                    pump_dq(4)
                for i in range(8):
                    nb = g * 8 + i
                    bi = i % 2
                    c_wg = dma("s", WG4[bi], wgt_c[nb],
                               sWg[bi], waits=[(sDqw, dq_done[("g", nb)]),
                                               (sT, wg_free[bi])])
                    c_wu = dma("s", WU4[bi], wut_c[nb],
                               sWu[bi], waits=[(sDqw, dq_done[("u", nb)]),
                                               (sT, wu_free[bi])])

                    gpo = (i % 2) * 512
                    upo = 1024 + (i % 2) * 512
                    prev = ((q4, g, i - 2) if i >= 2
                            else ((q4, g - 1, i + 6) if g > 0
                                  else ((q4 - 1, 15, i + 6) if q4 > 0 else None)))
                    tw = [(sWg[bi], c_wg), (sH2s, c_h2s)]
                    if prev in gelu_a:
                        tw.append((sA, gelu_a[prev]))

                    def gmm(t, bi=bi, gpo=gpo):
                        ins = None
                        for c in range(NCH):
                            ins = t.matmul(out=pbig[:, gpo:gpo + 512],
                                           lhsT=WG4[bi][:, c * 128:(c + 1) * 128],
                                           rhs=H2S[:, c * 512:(c + 1) * 512],
                                           start=(c == 0), stop=(c == NCH - 1))
                        return ins
                    c_gmm = put("t", tw, gmm, inc=sT)
                    wg_free[bi] = c_gmm
                    gate_mm[(q4, g, i)] = c_gmm

                    tw2 = [(sWu[bi], c_wu), (sH2s, c_h2s)]
                    if prev in inter_v:
                        tw2.append((sV, inter_v[prev]))

                    def umm(t, bi=bi, upo=upo):
                        ins = None
                        for c in range(NCH):
                            ins = t.matmul(out=pbig[:, upo:upo + 512],
                                           lhsT=WU4[bi][:, c * 128:(c + 1) * 128],
                                           rhs=H2S[:, c * 512:(c + 1) * 512],
                                           start=(c == 0), stop=(c == NCH - 1))
                        return ins
                    c_umm = put("t", tw2, umm, inc=sT)
                    wu_free[bi] = c_umm
                    h2s_users = c_umm

                    gel_prev = ((q4, g, i - 2) if i >= 2
                                else ((q4, g - 1, i + 6) if g > 0
                                      else ((q4 - 1, 15, i + 6) if q4 > 0 else None)))
                    tw3 = [(sT, c_gmm)]
                    if gel_prev in inter_v:
                        tw3.append((sV, inter_v[gel_prev]))
                    c_gel = put("a", tw3,
                                lambda a, bi=bi, gpo=gpo: a.activation(
                                    out=GEL[bi], in_=pbig[:, gpo:gpo + 512],
                                    func=AFT.Gelu_apprx_tanh), inc=sA)
                    gelu_a[(q4, g, i)] = c_gel

                    tw4 = [(sA, c_gel), (sT, c_umm)]
                    pg = ((q4, g - 1, 15) if g > 0
                          else ((q4 - 1, 15, 15) if q4 > 0 else None))
                    if pg in down_mm:
                        tw4.append((sT, down_mm[pg]))
                    c_int = put("v", tw4,
                                lambda v, bi=bi, i=i, upo=upo:
                                v.scalar_tensor_tensor(
                                    out=INTER[:, i * 512:(i + 1) * 512],
                                    in0=GEL[bi], scalar=1.0,
                                    in1=pbig[:, upo:upo + 512],
                                    op0=ALU.mult, op1=ALU.mult), inc=sV)
                    inter_v[(q4, g, i)] = c_int

                c_wd = 0
                for i in range(8):
                    nb = g * 8 + i
                    c_wd = dma("s", WD4[i], wdt_c[nb],
                               sWdl, waits=[(sDqw, dq_done[("dq", nb // 32)]),
                                            (sT, wd_free)])

                c_int_all = inter_v[(q4, g, 7)]
                for hc in range(NCH):
                    po = (hc % 2) * 512
                    pa = ((q4, g, hc - 2) if hc >= 2
                          else ((q4, g - 1, 14 + hc) if g > 0
                                else ((q4 - 1, 15, 14 + hc) if q4 > 0 else None)))
                    tw = [(sV, c_int_all), (sWdl, c_wd)]
                    if pa in acc_add:
                        tw.append((sV, acc_add[pa]))

                    def dmm(t, hc=hc, po=po):
                        ins = None
                        for i in range(8):
                            ins = t.matmul(out=pmid[:, po:po + 512],
                                           lhsT=WD4[i][:, hc * 128:(hc + 1) * 128],
                                           rhs=INTER[:, i * 512:(i + 1) * 512],
                                           start=(i == 0), stop=(i == 7))
                        return ins
                    c_dmm = put("t", tw, dmm, inc=sT)
                    down_mm[(q4, g, hc)] = c_dmm
                    wd_free = c_dmm

                    c_add = put("v", [(sT, c_dmm), (sAcc, c_acc)],
                                lambda v, hc=hc, po=po: v.tensor_tensor(
                                    out=ACC[:, hc * 512:(hc + 1) * 512],
                                    in0=ACC[:, hc * 512:(hc + 1) * 512],
                                    in1=pmid[:, po:po + 512], op=ALU.add),
                                inc=sV)
                    acc_add[(q4, g, hc)] = c_add

            for hc in range(NCH):
                out_done = dma("s", outT[hc * 128:(hc + 1) * 128, t0:t0 + 512],
                               ACC[:, hc * 512:(hc + 1) * 512],
                               sO, waits=[(sV, acc_add[(q4, 15, hc)])])
            acc_users = acc_add[(q4, 15, 15)]

        for dap, src in dbg:
            dma("s", dap, src, sO, waits=[(sO, sO.n)])

        put("s", [(sO, sO.n)], lambda s: s.nop(), inc=None)

        @block.gpsimd
        def _(g):
            for f in Q["g"]:
                f(g)

        @block.sync
        def _(s):
            for f in Q["s"]:
                f(s)

        @block.vector
        def _(v):
            for f in Q["v"]:
                f(v)

        @block.scalar
        def _(a):
            for f in Q["a"]:
                f(a)

        @block.tensor
        def _(t):
            for f in Q["t"]:
                f(t)

    return nc


def _sig_n():
    idx = []
    for q in range(4):
        for par in range(2):
            for cblk in range(16):
                base = 4096 * q + 256 * cblk + par
                idx += list(range(base, base + 256, 2))
    return np.array(idx, dtype=np.int64)


def make_inputs(inp, core):
    f32 = np.float32
    x = inp["L0_attn_x"]
    sig = _sig_n()
    bq = np.asarray(inp["L0_attn_bq"], f32)
    bkv = np.asarray(inp["L0_attn_bkv"], f32)
    bo = np.asarray(inp["L0_attn_bo"], f32)
    gq = np.asarray(inp["L0_attn_g_q"], f32)
    gkv = np.asarray(inp["L0_attn_g_kv"], f32)
    return {
        "xT": np.ascontiguousarray(np.asarray(x[core], f32).T),
        "wq": np.ascontiguousarray(np.asarray(inp["L0_attn_Wq"], f32)),
        "wkv": np.ascontiguousarray(np.asarray(inp["L0_attn_Wkv"], f32)),
        "wo": np.ascontiguousarray(np.asarray(inp["L0_attn_Wo"], f32)),
        "bq_t": np.ascontiguousarray(bq.reshape(16, 128).T),
        "bkvk_t": np.ascontiguousarray(bkv[:D].reshape(16, 128).T),
        "bo_t": np.ascontiguousarray(bo.reshape(16, 128).T),
        "bkv_v": np.ascontiguousarray(bkv[D:].reshape(1, D)),
        "gq_t": np.ascontiguousarray(gq.reshape(16, 128).T),
        "gkv_t": np.ascontiguousarray(gkv.reshape(16, 128).T),
        "ghead": np.ascontiguousarray(
            np.asarray(inp["L0_attn_g_head"], f32).reshape(128, 1)),
        "pkg": np.ascontiguousarray(
            np.asarray(inp["L0_ffn_gate_W_packed"], np.int32)[sig]),
        "scg": np.ascontiguousarray(
            np.asarray(inp["L0_ffn_gate_scales"], f32)[sig]),
        "pku": np.ascontiguousarray(
            np.asarray(inp["L0_ffn_up_W_packed"], np.int32)[sig]),
        "scu": np.ascontiguousarray(
            np.asarray(inp["L0_ffn_up_scales"], f32)[sig]),
        "pkd": np.ascontiguousarray(np.asarray(inp["L0_ffn_down_W_packed"], np.int32)),
        "scd": np.ascontiguousarray(np.asarray(inp["L0_ffn_down_scales"], f32)),
        "ident": np.eye(128, dtype=ml_dtypes.bfloat16),
        "ones_c_f": np.ones((128, 1), f32),
        "ones_r_f": np.ones((1, 128), f32),
        "ones_c_b": np.ones((128, 1), ml_dtypes.bfloat16),
        "ones_r_b": np.ones((1, 128), ml_dtypes.bfloat16),
        "epsv": np.array([[EPS1, EPS2]], np.float32),
    }


def kernel(**inputs):
    nc = build()
    in_maps = [make_inputs(inputs, core) for core in range(8)]
    res = run_bass_kernel_spmd(nc, in_maps, core_ids=list(range(8)))
    out = np.stack([np.asarray(r["outT"], dtype=np.float32).T
                    for r in res.results], axis=0)
    return np.ascontiguousarray(out)
